# version 02
# baseline (speedup 1.0000x reference)
import math
import numpy as np
import ml_dtypes
from contextlib import ExitStack
import concourse.bass as bass
import concourse.mybir as mybir
from concourse.bass_utils import run_bass_kernel_spmd

F32 = mybir.dt.float32
BF16 = mybir.dt.bfloat16
F16 = mybir.dt.float16
U8 = mybir.dt.uint8
AF = mybir.ActivationFunctionType
ALU = mybir.AluOpType
AX = mybir.AxisListType.X
RG8 = [[0, 1, 2, 3, 4, 5, 6, 7]]
BF = ml_dtypes.bfloat16

T = 1024
TH = 512
D = 2048
DC = 16
I2 = 11264
HC = 44
EPS_LN = 1e-5
EPS_RMS = 1e-6

BIG_COLS = 36864
A_H = 0
A_Y = 22528
A_SQ = 30720
A_KROT = 0
A_VROT = 8192
A_QY = 16384
A_RP = 32768
A_GBY = 0
A_DBY = 5632
A_DQ = 8448
A_STG = 22528


class Prog:
    ENGINES = ("sync", "vector", "scalar", "tensor", "gpsimd")

    def __init__(self, nc, ctx):
        self.nc = nc
        self.ctx = ctx
        self.q = {e: [] for e in self.ENGINES}
        self.sems = {}
        self.cnt = {}

    def sem(self, name):
        if name not in self.sems:
            self.sems[name] = self.ctx.enter_context(self.nc.semaphore(name))
            self.cnt[name] = 0
        return self.sems[name]

    def val(self, name):
        self.sem(name)
        return self.cnt[name]

    def op(self, eng, fn, waits=(), inc=None, n=1):
        waits = tuple((s, v) for (s, v) in waits if v and v > 0)
        for s, _ in waits:
            self.sem(s)
        tgt = None
        if inc is not None:
            self.sem(inc)
            self.cnt[inc] += n
            tgt = self.cnt[inc]
        self.q[eng].append((fn, waits, inc, n))
        return tgt

    def dma(self, eng, dst, src, waits=(), inc=None):
        if inc is None:
            inc = "ds" if eng == "sync" else "dg"
        return self.op(eng, lambda e, d=dst, s=src: e.dma_start(out=d, in_=s),
                       waits, inc, 16)

    def emit(self, block):
        for e in self.ENGINES:
            items = self.q[e]
            if not items:
                continue

            def body(eng, items=items):
                for fn, waits, inc, n in items:
                    for s, v in waits:
                        eng.wait_ge(self.sems[s], v)
                    inst = fn(eng)
                    if inc is not None:
                        inst.then_inc(self.sems[inc], n)
            getattr(block, e)(body)


def build_program(taps=()):
    taps = set(taps)
    nc = bass.Bass("TRN2", target_bir_lowering=False)

    def din(name, shape, dt):
        return nc.dram_tensor(name, shape, dt, kind="ExternalInput").ap()

    x_in = din("x_in", [T, D], F32)
    win_sl = din("win_sl", [256, 6144], F32)
    wout_sl = din("wout_sl", [256, 2048], F32)
    b_in = din("b_in", [48, 128], F32)
    b_out = din("b_out", [16, 128], F32)
    gam_in = din("gam_in", [16, 128], F32)
    bet_in = din("bet_in", [16, 128], F32)
    gup_by = din("gup_by", [I2, 128], U8)
    gup_sc = din("gup_sc", [352, 128], F16)
    dwn_by = din("dwn_by", [D, 352], U8)
    dwn_sc = din("dwn_sc", [D, 352], F16)
    rope_cos = din("rope_cos", [1024, T], F32)
    rope_sin = din("rope_sin", [1024, T], F32)
    ident_f32 = din("ident_f32", [128, 128], F32)
    perm_f32 = din("perm_f32", [128, 128], F32)
    onesrow_f32 = din("onesrow_f32", [1, 128], F32)
    ones_bf16 = din("ones_bf16", [128, 1], BF16)
    cvals = din("cvals", [128, 4], F32)

    y_out = nc.dram_tensor("y_out", [T, D], F32, kind="ExternalOutput").ap()
    tap = {}
    tapspec = {"wguT": ([D, I2], BF16), "wdT": ([5632, D], BF16),
               "wdP": ([704, D], BF16), "ln1h0": ([128, DC * TH], BF16),
               "hh0": ([128, HC * TH], BF16), "yh0": ([128, DC * TH], BF16),
               "xn": ([128, DC * TH], BF16), "kr": ([128, DC * TH], BF16),
               "vr": ([128, DC * TH], BF16), "qy": ([128, DC * T], BF16),
               "at": ([128, 16], F32), "ari": ([128, 80], F32),
               "aro": ([128, 80], F32),
               "winT": ([D, 6144], BF16), "woT": ([D, D], BF16),
               "ffn1": ([T, D], F32), "attn": ([T, D], F32)}
    for tn in taps:
        shp, dtp = tapspec[tn]
        tap[tn] = nc.dram_tensor("tap_" + tn, shp, dtp,
                                 kind="ExternalOutput").ap()

    def idram(name, shape, dt):
        return nc.dram_tensor(name, shape, dt, kind="Internal").ap()

    wgu_piece = idram("wgu_piece", [256, I2], BF16)
    wguT = idram("wguT", [D, I2], BF16)
    wd_piece = idram("wd_piece", [704, D], BF16)
    wdT = idram("wdT", [5632, D], BF16)
    win_piece = idram("win_piece", [256, 6144], BF16)
    winT = idram("winT", [D, 6144], BF16)
    wo_piece = idram("wo_piece", [256, D], BF16)
    woT = idram("woT", [D, D], BF16)
    arA_in = idram("arA_in", [128, 128], F32)
    arA_out = idram("arA_out", [1024, 128], F32)

    with ExitStack() as ctx:
        def sbt(name, shape, dt):
            return ctx.enter_context(nc.sbuf_tensor(name, shape, dt))

        def pst(name, shape, dt=F32):
            return ctx.enter_context(nc.psum_tensor(name, shape, dt))

        xfm = sbt("xfm", [128, DC * T], F32)
        big = sbt("big", [128, BIG_COLS], BF16)
        actb = sbt("actb", [128, DC * TH], BF16)
        wbuf = sbt("wbuf", [128, 11264], BF16)
        scr = sbt("scr", [128, 10 * TH], F32)
        bcast = sbt("bcast", [128, 2 * TH], F32)
        cI32 = sbt("cI32", [128, 128], F32)
        cPerm = sbt("cPerm", [128, 128], F32)
        cOnesR = sbt("cOnesR", [1, 128], F32)
        cOnes16 = sbt("cOnes16", [128, 1], BF16)
        cV = sbt("cV", [128, 4], F32)
        bias_q = sbt("bias_q", [128, 48], F32)
        bias_o = sbt("bias_o", [128, 16], F32)
        gamt = sbt("gamt", [128, 16], F32)
        bett = sbt("bett", [128, 16], F32)
        atile = sbt("atile", [128, 32], F32)
        atile2 = sbt("atile2", [128, 32], F32)
        aio = sbt("aio", [128, 80], F32)
        scT16 = sbt("scT16", [128, 352], F16)
        scTg = sbt("scTg", [128, 352], F32)


        p_mm = [pst(f"pmm{i}", [128, 512]) for i in range(4)]
        p_s1 = pst("ps1", [128, 512])
        p_s2 = pst("ps2", [128, 512])
        p_bc = pst("pbc", [128, 512])
        p_ax = pst("pax", [128, 512])

        P = Prog(nc, ctx)
        block = ctx.enter_context(nc.Block())

        SCR = lambda i: scr[:, i * TH:(i + 1) * TH]
        ROWS = [scr[0:1, (6 + i) * TH:(6 + i) * TH + TH] for i in range(3)]
        xio = actb[:].bitcast(F32)

        bank_free = {i: ("ev", 0) for i in range(4)}
        mm_last = {}

        P.dma("sync", cI32[:], ident_f32, inc="ldc")
        P.dma("sync", cPerm[:], perm_f32, inc="ldc")
        P.dma("sync", cOnesR[:], onesrow_f32, inc="ldc")
        P.dma("sync", cOnes16[:], ones_bf16, inc="ldc")
        P.dma("sync", cV[:], cvals, inc="ldc")
        ds_consts = ("ldc", P.val("ldc"))

        P.op("sync", lambda e: e.dma_start(out=scT16[:], in_=gup_sc,
                                           transpose=True), inc="ldc", n=16)
        ds_sc = P.val("ldc")
        gby = big[:, A_GBY:A_GBY + 5632].bitcast(U8)
        dby = big[:, A_DBY:A_DBY + 2816].bitcast(U8)
        P.dma("sync", gby.rearrange("p (oc b) -> p oc b", b=128),
              gup_by.rearrange("(oc p) b -> oc p b", p=128).transpose([1, 0, 2]),
              inc="ldb")
        P.dma("sync", dby.rearrange("p (oc b) -> p oc b", b=352),
              dwn_by.rearrange("(oc p) b -> oc p b", p=128).transpose([1, 0, 2]),
              inc="ldb")
        ds_bytes = ("ldb", P.val("ldb"))

        P.op("vector", lambda e: e.memset(aio[:, 16:80], 0.0), inc="ev")
        P.op("vector", lambda e: e.memset(atile[:, :], 0.0), inc="ev")
        P.op("vector", lambda e: e.memset(atile2[:, :], 0.0), inc="ev")
        ev_scg = P.op("vector", lambda e: e.tensor_copy(scTg[:], scT16[:]),
                      waits=[("ldc", ds_sc)], inc="ev")


        P.dma("gpsimd", win_piece, win_sl)
        P.dma("gpsimd", wo_piece, wout_sl)
        dg_wio = P.val("dg")

        xT_last = {}
        for tc in range(8):
            w = [("mm", xT_last.get(tc - 2, 0))]
            dsx = P.dma("sync", xio[:, (tc % 2) * D:(tc % 2) * D + D],
                        x_in[tc * 128:(tc + 1) * 128, :], waits=w,
                        inc=f"xs{tc % 2}")
            for dc in range(DC):
                b = (tc * DC + dc) % 4
                sem, v = bank_free[b]
                t = P.op("tensor",
                         lambda e, tc=tc, dc=dc, b=b:
                         e.transpose(p_mm[b][:, 0:128],
                                     xio[:, (tc % 2) * D + dc * 128:
                                         (tc % 2) * D + dc * 128 + 128],
                                     cI32[:]),
                         waits=[(f"xs{tc % 2}", dsx), (sem, v), ds_consts],
                         inc="mm")
                a = P.op("scalar",
                         lambda e, tc=tc, dc=dc, b=b:
                         e.copy(xfm[:, dc * T + tc * 128:dc * T + tc * 128 + 128],
                                p_mm[b][:, 0:128]),
                         waits=[("mm", t)], inc="ac")
                bank_free[b] = ("ac", a)
            xT_last[tc] = t
        ac_xdone = P.val("ac")

        btr_prev_mm = 0
        btr_prev_ac = 0

        def load_tr(src_dram, nrows, dst_sbuf):
            nonlocal btr_prev_mm, btr_prev_ac
            st = scr[0:nrows, 8 * TH:8 * TH + 128]
            dsv = P.dma("sync", st, src_dram, waits=[("mm", btr_prev_mm)],
                        inc="ldc")
            t = P.op("tensor", lambda e, st=st, nrows=nrows:
                     e.transpose(p_ax[0:128, 0:nrows], st,
                                 cI32[0:nrows, 0:nrows]),
                     waits=[("ldc", dsv), ("ac", btr_prev_ac),
                            ds_consts],
                     inc="mm")
            btr_prev_mm = t
            a = P.op("scalar", lambda e, dst_sbuf=dst_sbuf, nrows=nrows:
                     e.copy(dst_sbuf, p_ax[:, 0:nrows]),
                     waits=[("mm", t)], inc="ac")
            btr_prev_ac = a
            return a

        load_tr(b_in, 48, bias_q[:])
        load_tr(b_out, 16, bias_o[:])
        load_tr(gam_in, 16, gamt[:])
        load_tr(bet_in, 16, bett[:])
        mm_bias_done = P.val("mm")
        pax_free0 = ("ac", btr_prev_ac)

        dq = [big[:, A_DQ + i * 2816:A_DQ + (i + 1) * 2816] for i in range(5)]
        wv32 = scr[:, 0:2816]
        stg = big[:, A_STG:A_STG + I2]
        gather_done = {}
        pax_free = pax_free0
        pbc_free = ("ac", 0)

        def decode_batch(src_u8_flat, n, scale3, out_f32, pre=()):
            w0 = list(pre)
            e1 = P.op("vector", lambda e: e.tensor_copy(dq[0][:, 0:n],
                                                        src_u8_flat),
                      waits=w0, inc="ev")
            P.op("vector", lambda e: e.tensor_single_scalar(
                dq[1][:, 0:n], dq[0][:, 0:n], 8.0, ALU.is_ge), inc="ev")
            P.op("vector", lambda e: e.scalar_tensor_tensor(
                dq[2][:, 0:n], dq[1][:, 0:n], -8.0, dq[0][:, 0:n],
                ALU.mult, ALU.add), inc="ev")
            ev_m = P.cnt["ev"]
            a1 = P.op("scalar", lambda e: e.activation(
                dq[3][:, 0:n], dq[2][:, 0:n], AF.Relu, bias=cV[:, 0:1]),
                waits=[("ev", ev_m), ds_consts], inc="ac")
            a2 = P.op("scalar", lambda e: e.activation(
                dq[4][:, 0:n], dq[2][:, 0:n], AF.Relu, bias=cV[:, 1:2]),
                inc="ac")
            P.op("vector", lambda e: e.scalar_tensor_tensor(
                dq[0][:, 0:n], dq[2][:, 0:n], 0.5, dq[4][:, 0:n],
                ALU.mult, ALU.add), waits=[("ac", a2)], inc="ev")
            P.op("vector", lambda e: e.scalar_tensor_tensor(
                dq[2][:, 0:n], dq[3][:, 0:n], 0.5, dq[0][:, 0:n],
                ALU.mult, ALU.add), waits=[("ac", a1)], inc="ev")
            P.op("vector", lambda e: e.tensor_mul(
                dq[0][:, 0:n], dq[1][:, 0:n], dq[2][:, 0:n]), inc="ev")
            P.op("vector", lambda e: e.scalar_tensor_tensor(
                dq[1][:, 0:n], dq[0][:, 0:n], -2.0, dq[2][:, 0:n],
                ALU.mult, ALU.add), inc="ev")
            sc_ap, out3 = scale3
            in3 = dq[1][:, 0:n].rearrange("p (oc b) -> p oc b",
                                          b=sc_ap.shape[2])
            evs = P.op("vector", lambda e: e.tensor_tensor(
                out3, in3, sc_ap, ALU.mult), inc="ev")
            return evs

        prev_tr = 0
        prev_dma_par = {0: 0, 1: 0}
        for j in range(4):
            for par in range(2):
                src = gby.rearrange("p (oc b) -> p oc b", b=128)[
                    :, :, j * 32:(j + 1) * 32]
                nib_u8 = dq[4][:, 0:1408].bitcast(U8)
                nib3 = nib_u8.rearrange("p (oc b) -> p oc b", b=32)
                if par == 0:
                    e0 = P.op("vector", lambda e, src=src, nib3=nib3:
                              e.tensor_single_scalar(nib3, src, 15,
                                                     ALU.bitwise_and),
                              waits=[ds_bytes], inc="ev")
                else:
                    e0 = P.op("vector", lambda e, src=src, nib3=nib3:
                              e.tensor_single_scalar(nib3, src, 4,
                                                     ALU.logical_shift_right),
                              waits=[ds_bytes], inc="ev")
                sc_b = scTg[:, j * 88:(j + 1) * 88].unsqueeze(2).broadcast_to([128, 88, 32])
                wv3 = wv32.rearrange("p (oc b) -> p oc b", b=32)
                evs = decode_batch(nib_u8, 2816, (sc_b, wv3), wv32,
                                   pre=[("ev", e0), ("ev", ev_scg),
                                        ("mm", mm_bias_done),
                                        ("mm", prev_tr)])
                for g4 in range(22):
                    pb = p_ax if g4 % 2 == 0 else p_bc
                    pfree = pax_free if g4 % 2 == 0 else pbc_free
                    for oi in range(4):
                        oc = g4 * 4 + oi
                        tmm = P.op("tensor",
                                   lambda e, oc=oc, oi=oi, pb=pb:
                                   e.transpose(pb[0:32, oi * 128:(oi + 1) * 128],
                                               wv32[:, oc * 32:(oc + 1) * 32],
                                               cI32[:]),
                                   waits=([("ev", evs), pfree]
                                          if oi == 0 else ()),
                                   inc=("mm" if oi == 3 else None))
                    oc0 = g4 * 4
                    c0 = oc0 if oc0 < 44 else oc0 - 44
                    off = 0 if oc0 < 44 else 128
                    dst = stg[par * 32:par * 32 + 32, :].rearrange(
                        "p (c u2) -> p c u2", u2=256)[:, c0:c0 + 4, off:off + 128]
                    srcp = pb[0:32, 0:512].rearrange("p (c u) -> p c u", u=128)
                    aev = P.op("scalar",
                               lambda e, dst=dst, srcp=srcp: e.copy(dst, srcp),
                               waits=[("mm", tmm),
                                      (f"sg{par}", prev_dma_par[par])],
                               inc="ac")
                    if g4 % 2 == 0:
                        pax_free = ("ac", aev)
                    else:
                        pbc_free = ("ac", aev)
                prev_tr = tmm
                P.op("scalar", lambda e: e.copy(bcast[0:1, 0:4],
                                                cV[0:1, 0:4]), inc="ac")
                ac_par = P.val("ac")
                prev_dma_par[par] = P.dma(
                    "gpsimd", wgu_piece[j * 64 + par:(j + 1) * 64:2, :],
                    stg[par * 32:par * 32 + 32, :],
                    waits=[("ac", ac_par)], inc=f"sg{par}")
            ccj = P.op("gpsimd", lambda e, j=j: e.collective_compute(
                "AllGather", ALU.bypass, replica_groups=RG8,
                ins=[wgu_piece[j * 64:(j + 1) * 64, :].opt()],
                outs=[wguT[j * 512:(j + 1) * 512, :].opt()]),
                waits=[("sg0", P.val("sg0")), ("sg1", P.val("sg1"))],
                inc="cc")
            gather_done[("wgu", j)] = ccj

        if "wguT" in tap:
            P.dma("gpsimd", tap["wguT"][:, :], wguT[:, :],
                  waits=[("cc", gather_done[("wgu", 3)])])

        dsc16v = big[:, A_DQ:A_DQ + 5632].bitcast(F16)
        dscE32 = big[:, A_STG:A_STG + 11264].bitcast(F32)
        dg_wgu_done = P.val("dg")
        ev_wgu_done = P.val("ev")
        ac_wgu_done = P.val("ac")
        P.dma("sync", dsc16v.rearrange("p (oc b) -> p oc b", b=352),
              dwn_sc.rearrange("(oc p) b -> oc p b", p=128).transpose([1, 0, 2]),
              waits=[("ev", ev_wgu_done), ("ac", ac_wgu_done)], inc="ldd")
        ds_dsc = P.val("ldd")
        ev_dsc = P.op("vector", lambda e: e.tensor_copy(dscE32, dsc16v),
                      waits=[("ldd", ds_dsc), ("sg0", P.val("sg0")),
                             ("sg1", P.val("sg1"))], inc="ev")
        import os as _os
        SER = int(_os.environ.get("SER_WD", "0"))

        def allw():
            return [(k, P.val(k)) for k in ("ds", "dg", "ev", "ac", "mm")]
        tmp_bf = [scr[:, 2816 + i * 64:2816 + (i + 1) * 64].bitcast(BF16)
                  for i in range(3)]
        tmp_dma = {0: 0, 1: 0, 2: 0}
        wd_pend = []
        dby3 = dby.rearrange("p (oc b) -> p oc b", b=352)
        dsc3 = dscE32.rearrange("p (oc b) -> p oc b", b=352)
        ti = 0
        for par in range(2):
            for h8 in range(2):
                src = dby3[:, h8 * 8:(h8 + 1) * 8, :]
                nib_u8 = dq[4][:, 0:1408].bitcast(U8)
                nib3 = nib_u8.rearrange("p (oc b) -> p oc b", b=352)
                if par == 0:
                    e0 = P.op("vector", lambda e, src=src, nib3=nib3:
                              e.tensor_single_scalar(nib3, src, 15,
                                                     ALU.bitwise_and),
                              waits=[ds_bytes], inc="ev")
                else:
                    e0 = P.op("vector", lambda e, src=src, nib3=nib3:
                              e.tensor_single_scalar(nib3, src, 4,
                                                     ALU.logical_shift_right),
                              waits=[ds_bytes], inc="ev")
                sc3 = dsc3[:, h8 * 8:(h8 + 1) * 8, :]
                wv3 = wv32.rearrange("p (oc b) -> p oc b", b=352)
                evs = decode_batch(nib_u8, 2816, (sc3, wv3), wv32,
                                   pre=[("ev", e0), ("ev", ev_dsc),
                                        ("mm", prev_tr)] +
                                   (allw() if SER in (2, 3) else []))
                for oc8 in range(8):
                    oc = h8 * 8 + oc8
                    pb = p_ax if oc8 % 2 == 0 else p_bc
                    pfree = pax_free if oc8 % 2 == 0 else pbc_free
                    for ss in range(3):
                        cnt = 352 - ss * 128 if ss == 2 else 128
                        wts = [("ev", evs), pfree] if ss == 0 else []
                        if SER != 0 and ss == 0:
                            wts += [("ac", P.val("ac")), ("dg", P.val("dg"))]
                        if SER == 2:
                            wts = wts + allw()
                        tmm = P.op(
                            "tensor",
                            lambda e, oc8=oc8, ss=ss, cnt=cnt, pb=pb:
                            e.transpose(pb[0:cnt, ss * 128:ss * 128 + 128],
                                        wv32[:, oc8 * 352 + ss * 128:
                                             oc8 * 352 + ss * 128 + cnt],
                                        cI32[:]),
                            waits=wts,
                            inc=("mm" if ss == 2 else None))
                    aevl = None
                    for ss in range(3):
                        cnt = 352 - ss * 128 if ss == 2 else 128
                        tsl = ti % 3
                        aevl = P.op(
                            "scalar",
                            lambda e, ss=ss, cnt=cnt, tsl=tsl, pb=pb:
                            e.copy(tmp_bf[tsl][0:cnt, :],
                                   pb[0:cnt, ss * 128:ss * 128 + 128]),
                            waits=[("mm", tmm),
                                   (f"tp{tsl}", tmp_dma[tsl])],
                            inc="ac")
                        wd_pend.append(
                            (par, ss, cnt, oc, tsl))
                        while len(wd_pend) > 2:
                            q = wd_pend.pop(0)
                            tmp_dma[q[4]] = P.dma(
                                "gpsimd",
                                wd_piece[q[0] + 256 * q[1]:
                                         min(q[0] + 256 * q[1] + 2 * q[2],
                                             704):2,
                                         q[3] * 128:(q[3] + 1) * 128],
                                tmp_bf[q[4]][0:q[2], :],
                                waits=[("ac", aevl)], inc=f"tp{q[4]}")
                        ti += 1
                    if oc8 % 2 == 0:
                        pax_free = ("ac", aevl)
                    else:
                        pbc_free = ("ac", aevl)
                prev_tr = tmm
        for _ in range(2):
            aevl = P.op("scalar", lambda e: e.copy(bcast[0:1, 0:4],
                                                   cV[0:1, 0:4]), inc="ac")
        while wd_pend:
            q = wd_pend.pop(0)
            tmp_dma[q[4]] = P.dma(
                "gpsimd",
                wd_piece[q[0] + 256 * q[1]:
                         min(q[0] + 256 * q[1] + 2 * q[2], 704):2,
                         q[3] * 128:(q[3] + 1) * 128],
                tmp_bf[q[4]][0:q[2], :],
                waits=[("ac", aevl)], inc=f"tp{q[4]}")
        dg_wd = P.val("dg")
        cc_wd = P.op("gpsimd", lambda e: e.collective_compute(
            "AllGather", ALU.bypass, replica_groups=RG8,
            ins=[wd_piece[:, :].opt()], outs=[wdT[:, :].opt()]),
            waits=[("tp0", P.val("tp0")), ("tp1", P.val("tp1")),
                   ("tp2", P.val("tp2"))], inc="cc")
        cc_win = P.op("gpsimd", lambda e: e.collective_compute(
            "AllGather", ALU.bypass, replica_groups=RG8,
            ins=[win_piece[:, :].opt()], outs=[winT[:, :].opt()]),
            waits=[("dg", dg_wio)], inc="cc")
        cc_wo = P.op("gpsimd", lambda e: e.collective_compute(
            "AllGather", ALU.bypass, replica_groups=RG8,
            ins=[wo_piece[:, :].opt()], outs=[woT[:, :].opt()]),
            waits=[("dg", dg_wio)], inc="cc")
        if "wdT" in tap:
            P.dma("gpsimd", tap["wdT"][:, :], wdT[:, :],
                  waits=[("cc", cc_wd)])
        if "wdP" in tap:
            P.dma("gpsimd", tap["wdP"][:, :], wd_piece[:, :],
                  waits=[("tp0", P.val("tp0")), ("tp1", P.val("tp1")),
                         ("tp2", P.val("tp2"))])
        if "winT" in tap:
            P.dma("gpsimd", tap["winT"][:, :], winT[:, :],
                  waits=[("cc", cc_win)])

        RINV_D = 1.0 / 2048.0
        pbc_sb_free = ("ev", 0)

        def stats_pass(half, xb_base, tag):
            nonlocal ps_free
            """sum/sumsq over features of xfm half -> p_s1[0:1], p_s2[0:1].

            xb_base: 4 bf16 scratch slots (big cols) for xb / xsq.
            Returns mm value of the last stats matmul."""
            ev_xb = {}
            for c in range(DC):
                sl = c % 2
                xb = big[:, xb_base + sl * 512:xb_base + sl * 512 + 512]
                xq = big[:, xb_base + 1024 + sl * 512:
                         xb_base + 1024 + sl * 512 + 512]
                w = [("mm", mm_stats.get(("s", tag, half, c - 2), 0))]
                P.op("vector", lambda e, c=c, xb=xb:
                     e.tensor_copy(xb, xfm[:, c * T + half * TH:
                                           c * T + half * TH + TH]),
                     waits=w, inc="ev")
                evq = P.op("vector", lambda e, xb=xb, xq=xq:
                           e.tensor_mul(xq, xb, xb), inc="ev")
                P.op("tensor", lambda e, c=c, xb=xb:
                     e.matmul(p_s1[0:1, :], cOnes16[:], xb,
                              start=(c == 0), stop=(c == DC - 1)),
                     waits=[("ev", evq)] + ([ps_free] if c == 0 else []),
                     inc=None)
                m2 = P.op("tensor", lambda e, c=c, xq=xq:
                          e.matmul(p_s2[0:1, :], cOnes16[:], xq,
                                   start=(c == 0), stop=(c == DC - 1)),
                          inc="mm")
                mm_stats[("s", tag, half, c)] = m2
            return m2

        bc_free = pbc_free

        def bcast_row(row_ap, dst_idx, wait_ev):
            nonlocal bc_free, pbc_sb_free
            m = P.op("tensor", lambda e, row_ap=row_ap:
                     e.matmul(p_bc[:, :], cOnesR[:], row_ap,
                              start=True, stop=True),
                     waits=[("ev", wait_ev), bc_free, ds_consts],
                     inc="mm")
            a = P.op("scalar", lambda e, dst_idx=dst_idx:
                     e.copy(bcast[:, dst_idx * TH:(dst_idx + 1) * TH],
                            p_bc[:, :]),
                     waits=[("mm", m), pbc_sb_free], inc="ac")
            bc_free = ("ac", a)
            return a

        mm_stats = {}
        wslot_free = {0: ("cc", 0), 1: ("cc", 0)}
        ps_free = ("ev", 0)

        def ln_rows(eps_col, mm_wait):
            nonlocal ps_free
            ev0 = P.op("vector", lambda e: e.tensor_scalar_mul(
                ROWS[0], p_s1[0:1, :], RINV_D),
                waits=[("mm", mm_wait)], inc="ev")
            evp = P.op("vector", lambda e: e.tensor_scalar_mul(
                ROWS[1], p_s2[0:1, :], RINV_D), inc="ev")
            ps_free = ("ev", evp)
            P.op("vector", lambda e: e.tensor_mul(ROWS[2], ROWS[0], ROWS[0]),
                 inc="ev")
            evv = P.op("vector", lambda e: e.tensor_sub(ROWS[1], ROWS[1],
                                                        ROWS[2]), inc="ev")
            aq = P.op("scalar", lambda e, eps_col=eps_col: e.activation(
                ROWS[2], ROWS[1], AF.Sqrt, bias=cV[0:1, eps_col:eps_col + 1]),
                waits=[("ev", evv)], inc="ac")
            evr = P.op("vector", lambda e: e.reciprocal(ROWS[1], ROWS[2]),
                       waits=[("ac", aq)], inc="ev")
            return evr

        FFN_UPTO = int(_os.environ.get("FFN_UPTO", "4"))

        def ffn_call(ready, tag):
            nonlocal bc_free, pbc_sb_free, ps_free
            last_ev = 0
            for half in range(int(_os.environ.get("FFN_HALVES", "2"))):
                P.op("vector", lambda e: e.engine_nop(), waits=[ready],
                     inc="ev")
                if _os.environ.get("FFN_BARRIER", "0") == "1":
                    aw = [(k, P.val(k)) for k in
                          ("ds", "dg", "ev", "ac", "mm", "ws0", "ws1")]
                    P.op("vector", lambda e: e.engine_nop(), waits=aw,
                         inc="ev")
                    P.op("scalar", lambda e: e.copy(bcast[0:1, 0:4],
                                                    cV[0:1, 0:4]),
                         waits=aw, inc="ac")
                    P.op("tensor", lambda e: e.transpose(
                        p_ax[:, 0:128], cI32[:], cI32[:]), waits=aw, inc="mm")
                mstat = stats_pass(half, A_Y, tag)
                evr = ln_rows(2, mstat)
                a0 = bcast_row(ROWS[0], 0, evr)
                a1 = bcast_row(ROWS[1], 1, evr)
                ev_apply = 0
                for c in range(DC):
                    P.op("vector", lambda e, c=c, half=half: e.tensor_sub(
                        SCR(4), xfm[:, c * T + half * TH:c * T + half * TH + TH],
                        bcast[:, 0:TH]),
                        waits=([("ac", a1)] if c == 0 else ()), inc="ev")
                    ev_apply = P.op("vector", lambda e, c=c: e.tensor_mul(
                        actb[:, c * TH:(c + 1) * TH], SCR(4),
                        bcast[:, TH:2 * TH]), inc="ev")
                pbc_sb_free = ("ev", ev_apply)
                if tag == "ffn1" and half == 0 and "ln1h0" in tap:
                    P.dma("sync", tap["ln1h0"][:, :], actb[:, :],
                          waits=[("ev", ev_apply)])
                if FFN_UPTO <= 1:
                    last_ev = ev_apply
                    continue
                DMAONLY = FFN_UPTO == 15
                silu_done = {}
                h_done = {}
                for c in range(HC):
                    sl = c % 2
                    wofs = sl * 5632
                    wsem, wval = wslot_free[sl]
                    first_use = (tag == "ffn1" and half == 0 and c == 0)
                    dsv = 0
                    for kc in range(DC):
                        ccw = ([("cc", gather_done[("wgu", kc // 4)])]
                               if first_use else [])
                        dsv = P.dma("sync",
                                    wbuf[:, wofs + kc * 256:wofs + kc * 256 + 256],
                                    wguT[kc * 128:(kc + 1) * 128,
                                         256 * c:256 * c + 256],
                                    waits=([(wsem, wval)] if kc == 0 else [])
                                    + ccw,
                                    inc=f"ws{sl}")
                    if DMAONLY:
                        wslot_free[sl] = (f"ws{sl}", dsv)
                        continue
                    gb = p_mm[2 * sl]
                    ub = p_mm[2 * sl + 1]
                    for kc in range(DC):
                        wts = [(f"ws{sl}", dsv), ("ev", ev_apply)] if kc == 0 else []
                        if kc == 0:
                            wts.append(bank_free[2 * sl])
                        mg = P.op("tensor", lambda e, kc=kc, wofs=wofs, gb=gb:
                                  e.matmul(gb[:, :],
                                           wbuf[:, wofs + kc * 256:
                                                wofs + kc * 256 + 128],
                                           actb[:, kc * TH:(kc + 1) * TH],
                                           start=(kc == 0), stop=(kc == DC - 1)),
                                  waits=wts,
                                  inc=("mm" if kc == DC - 1 else None))
                    for kc in range(DC):
                        wts = [bank_free[2 * sl + 1]] if kc == 0 else []
                        mu_ = P.op("tensor", lambda e, kc=kc, wofs=wofs, ub=ub:
                                   e.matmul(ub[:, :],
                                            wbuf[:, wofs + kc * 256 + 128:
                                                 wofs + kc * 256 + 256],
                                            actb[:, kc * TH:(kc + 1) * TH],
                                            start=(kc == 0), stop=(kc == DC - 1)),
                                   waits=wts,
                                   inc=("mm" if kc == DC - 1 else None))
                    wslot_free[sl] = ("mm", mu_)
                    sg = big[:, A_SQ + sl * 512:A_SQ + sl * 512 + 512]
                    asl = P.op("scalar", lambda e, sg=sg, gb=gb:
                               e.activation(sg, gb[:, :], AF.Silu),
                               waits=[("mm", mg), ("ev", h_done.get(c - 2, 0))],
                               inc="ac")
                    bank_free[2 * sl] = ("ac", asl)
                    evh = P.op("vector", lambda e, c=c, sg=sg, ub=ub:
                               e.tensor_mul(big[:, A_H + c * TH:
                                                A_H + (c + 1) * TH],
                                            sg, ub[:, :]),
                               waits=[("ac", asl), ("mm", mu_)], inc="ev")
                    h_done[c] = evh
                    bank_free[2 * sl + 1] = ("ev", evh)
                ev_h_all = P.val("ev")
                if tag == "ffn1" and half == 0 and "hh0" in tap:
                    P.dma("sync", tap["hh0"][:, :],
                          big[:, A_H:A_H + HC * TH], waits=[("ev", ev_h_all)])
                if DMAONLY:
                    last_ev = ev_apply
                    continue
                if FFN_UPTO <= 2:
                    last_ev = ev_h_all
                    continue
                ysq_pend = []
                for pr in range(8):
                    slA, slB = 0, 1
                    wsem, wval = wslot_free[0]
                    ds_a = 0
                    for kc in range(22):
                        ds_a = P.dma("sync",
                                     wbuf[:, kc * 256:kc * 256 + 256],
                                     wdT[kc * 128:(kc + 1) * 128,
                                         256 * pr:256 * pr + 256],
                                     waits=([(wsem, wval), ("cc", cc_wd)]
                                            if kc == 0 else ()), inc="ws0")
                    wsem, wval = wslot_free[1]
                    ds_b = 0
                    for kc in range(22, HC):
                        ds_b = P.dma("sync",
                                     wbuf[:, 5632 + (kc - 22) * 256:
                                          5632 + (kc - 22) * 256 + 256],
                                     wdT[kc * 128:(kc + 1) * 128,
                                         256 * pr:256 * pr + 256],
                                     waits=([(wsem, wval)] if kc == 22 else ()),
                                     inc="ws1")
                    b0 = p_mm[2 * (pr % 2)]
                    b1 = p_mm[2 * (pr % 2) + 1]
                    m0 = m1_ = None
                    for kc in range(HC):
                        wofs = (kc // 22) * 5632
                        kl = kc % 22
                        rhs = big[:, A_H + kc * TH:A_H + (kc + 1) * TH]
                        wts0 = []
                        if kc == 0:
                            wts0 = [("ws0", ds_a), ("ev", ev_h_all),
                                    bank_free[2 * (pr % 2)]]
                        elif kc == 22:
                            wts0 = [("ws1", ds_b)]
                        m0 = P.op("tensor", lambda e, kl=kl, wofs=wofs, b0=b0,
                                  rhs=rhs, kc=kc:
                                  e.matmul(b0[:, :],
                                           wbuf[:, wofs + kl * 256:
                                                wofs + kl * 256 + 128],
                                           rhs, start=(kc == 0),
                                           stop=(kc == HC - 1)),
                                  waits=wts0,
                                  inc=("mm" if kc == HC - 1 else None))
                        wts1 = [bank_free[2 * (pr % 2) + 1]] if kc == 0 else []
                        m1_ = P.op("tensor", lambda e, kl=kl, wofs=wofs, b1=b1,
                                   rhs=rhs, kc=kc:
                                   e.matmul(b1[:, :],
                                            wbuf[:, wofs + kl * 256 + 128:
                                                 wofs + kl * 256 + 256],
                                            rhs, start=(kc == 0),
                                            stop=(kc == HC - 1)),
                                   waits=wts1,
                                   inc=("mm" if kc == HC - 1 else None))
                    wslot_free[0] = ("mm", m1_)
                    wslot_free[1] = ("mm", m1_)
                    while ysq_pend:
                        oc_, xq_, ev_ = ysq_pend.pop(0)
                        mm_stats[("y", oc_)] = P.op(
                            "tensor", lambda e, xq_=xq_, oc_=oc_:
                            e.matmul(p_s1[0:1, :], cOnes16[:], xq_,
                                     start=(oc_ == 0), stop=(oc_ == 15)),
                            waits=[("ev", ev_)] +
                            ([ps_free] if oc_ == 0 else []), inc="mm")
                    for oi, (bk, mmv) in enumerate(((b0, m0), (b1, m1_))):
                        oc = 2 * pr + oi
                        yt = big[:, A_Y + oc * TH:A_Y + (oc + 1) * TH]
                        ay = P.op("scalar", lambda e, yt=yt, bk=bk:
                                  e.copy(yt, bk[:, :]),
                                  waits=[("mm", mmv)], inc="ac")
                        bank_free[2 * (pr % 2) + oi] = ("ac", ay)
                        xq = big[:, A_SQ + oi * 512:A_SQ + (oi + 1) * 512]
                        evq = P.op("vector", lambda e, yt=yt, xq=xq:
                                   e.tensor_mul(xq, yt, yt),
                                   waits=[("ac", ay),
                                          ("mm", mm_stats.get(("y", oc - 2), 0))],
                                   inc="ev")
                        ysq_pend.append((oc, xq, evq))
                while ysq_pend:
                    oc_, xq_, ev_ = ysq_pend.pop(0)
                    mst = P.op("tensor", lambda e, xq_=xq_, oc_=oc_:
                               e.matmul(p_s1[0:1, :], cOnes16[:], xq_,
                                        start=(oc_ == 0), stop=(oc_ == 15)),
                               waits=[("ev", ev_)], inc="mm")
                    mm_stats[("y", oc_)] = mst
                if FFN_UPTO <= 3:
                    last_ev = P.val("ev")
                    continue
                if tag == "ffn1" and half == 0 and "yh0" in tap:
                    P.dma("sync", tap["yh0"][:, :],
                          big[:, A_Y:A_Y + DC * TH], waits=[("mm", mst)])
                ev0 = P.op("vector", lambda e: e.tensor_scalar_mul(
                    ROWS[0], p_s1[0:1, :], RINV_D),
                    waits=[("mm", mst)], inc="ev")
                ps_free = ("ev", ev0)
                aq = P.op("scalar", lambda e: e.activation(
                    ROWS[1], ROWS[0], AF.Sqrt, bias=cV[0:1, 3:4]),
                    waits=[("ev", ev0)], inc="ac")
                evq = P.op("vector", lambda e: e.reciprocal(ROWS[2], ROWS[1]),
                           waits=[("ac", aq)], inc="ev")
                evw = P.op("vector", lambda e: e.tensor_scalar_mul(
                    ROWS[0], ROWS[2], 0.5), inc="ev")
                a0 = bcast_row(ROWS[0], 0, evw)
                for oc in range(DC):
                    yt = big[:, A_Y + oc * TH:A_Y + (oc + 1) * TH]
                    P.op("vector", lambda e, yt=yt: e.tensor_mul(
                        SCR(4), yt, bcast[:, 0:TH]),
                        waits=([("ac", a0)] if oc == 0 else ()), inc="ev")
                    xsl = xfm[:, oc * T + half * TH:oc * T + half * TH + TH]
                    last_ev = P.op("vector", lambda e, xsl=xsl: e.tensor_add(
                        xsl, xsl, SCR(4)), inc="ev")
                pbc_sb_free = ("ev", last_ev)
                for oc in range(DC):
                    mm_stats.pop(("y", oc), None)
            return ("ev", last_ev)

        def xdump(dst_dram, gate):
            pend = []
            for tc in range(8):
                for dc in range(DC):
                    i = tc * DC + dc
                    b = i % 4
                    sl = i % 3
                    stile = scr[:, (2 + sl) * TH:(2 + sl) * TH + 128]
                    semv = bank_free[b]
                    t = P.op("tensor", lambda e, tc=tc, dc=dc, b=b:
                             e.transpose(p_mm[b][:, 0:128],
                                         xfm[:, dc * T + tc * 128:
                                             dc * T + tc * 128 + 128],
                                         cI32[:]),
                             waits=[gate, semv], inc="mm")
                    a = P.op("scalar",
                             lambda e, b=b, stile=stile:
                             e.copy(stile, p_mm[b][:, 0:128]),
                             waits=[("mm", t),
                                    (f"to{sl}", P.val(f"to{sl}"))],
                             inc="ac")
                    bank_free[b] = ("ac", a)
                    pend.append((tc, dc, sl, stile))
                    while len(pend) > 2:
                        q = pend.pop(0)
                        P.dma("sync",
                              dst_dram[q[0] * 128:(q[0] + 1) * 128,
                                       q[1] * 128:(q[1] + 1) * 128],
                              q[3], waits=[("ac", a)], inc=f"to{q[2]}")
            for _ in range(2):
                a = P.op("scalar", lambda e: e.copy(bcast[0:1, 0:4],
                                                    cV[0:1, 0:4]), inc="ac")
            while pend:
                q = pend.pop(0)
                P.dma("sync",
                      dst_dram[q[0] * 128:(q[0] + 1) * 128,
                               q[1] * 128:(q[1] + 1) * 128],
                      q[3], waits=[("ac", a)], inc=f"to{q[2]}")

        if _os.environ.get("SKIP_FFN", "0") == "1":
            ffn1_done = ("ac", ac_xdone)
        else:
            ffn1_done = ffn_call(("ac", ac_xdone), "ffn1")
        if "ffn1" in tap:
            xdump(tap["ffn1"], ffn1_done)

        def attn_call(ready):
            nonlocal bc_free, pbc_sb_free, ps_free, pax_free
            q_ac = {}
            P.op("vector", lambda e: e.memset(SCR(5)[:, 0:128], 0.0),
                 waits=[ready], inc="ev")
            sqK = scr[:, 9 * TH:9 * TH + TH].bitcast(BF16)[:, 0:512]
            sqV = scr[:, 9 * TH:9 * TH + TH].bitcast(BF16)[:, 512:1024]
            rope = [big[:, A_RP + k * 1024:A_RP + (k + 1) * 1024].bitcast(F32)
                    for k in range(4)]
            rope_user = {k: [] for k in range(4)}
            rope_cnt = {k: 0 for k in range(4)}
            ev_apply_h = {}
            ev_Adone = 0
            for half in range(2):
                P.op("vector", lambda e: e.engine_nop(), waits=[ready],
                     inc="ev")
                mstat = stats_pass(half, A_KROT, "attn")
                ev0 = P.op("vector", lambda e: e.tensor_scalar_mul(
                    ROWS[0], p_s1[0:1, :], RINV_D),
                    waits=[("mm", mstat)], inc="ev")
                evp = P.op("vector", lambda e: e.tensor_scalar_mul(
                    ROWS[1], p_s2[0:1, :], RINV_D), inc="ev")
                ps_free = ("ev", evp)
                P.op("vector", lambda e: e.tensor_mul(ROWS[2], ROWS[0],
                                                      ROWS[0]), inc="ev")
                evv = P.op("vector", lambda e: e.tensor_sub(
                    ROWS[1], ROWS[1], ROWS[2]), inc="ev")
                aq = P.op("scalar", lambda e: e.activation(
                    ROWS[2], ROWS[1], AF.Sqrt, bias=cV[0:1, 2:3]),
                    waits=[("ev", evv)], inc="ac")
                evr = P.op("vector", lambda e: e.reciprocal(ROWS[2], ROWS[2]),
                           waits=[("ac", aq)], inc="ev")
                a0 = bcast_row(ROWS[0], 0, evr)
                a1 = bcast_row(ROWS[2], 1, evr)
                P.op("vector", lambda e: e.tensor_mul(ROWS[1], ROWS[1],
                                                      ROWS[2]), inc="ev")
                evv2 = P.op("vector", lambda e: e.tensor_mul(
                    ROWS[1], ROWS[1], ROWS[2]), inc="ev")
                aq2 = P.op("scalar", lambda e: e.activation(
                    ROWS[1], ROWS[1], AF.Sqrt, bias=cV[0:1, 2:3]),
                    waits=[("ev", evv2)], inc="ac")
                evr2 = P.op("vector", lambda e: e.reciprocal(ROWS[1], ROWS[1]),
                            waits=[("ac", aq2)], inc="ev")
                evs12 = P.op("vector", lambda e: e.tensor_mul(
                    ROWS[1], ROWS[1], ROWS[2]), inc="ev")
                mbs = P.op("tensor", lambda e: e.matmul(
                    p_bc[:, :], cOnesR[:], ROWS[1], start=True, stop=True),
                    waits=[("ev", evs12), bc_free, ds_consts], inc="mm")
                ev_apply = 0
                for c in range(DC):
                    xsl = xfm[:, c * T + half * TH:c * T + half * TH + TH]
                    P.op("vector", lambda e, xsl=xsl: e.tensor_sub(
                        SCR(0), xsl, bcast[:, 0:TH]),
                        waits=([("ac", a1), ("mm", mbs)] if c == 0 else ()),
                        inc="ev")
                    P.op("vector", lambda e: e.tensor_mul(
                        SCR(1), SCR(0), bcast[:, TH:2 * TH]), inc="ev")
                    P.op("vector", lambda e, xsl=xsl: e.tensor_add(
                        xsl, xsl, SCR(1)), inc="ev")
                    P.op("vector", lambda e: e.tensor_mul(
                        SCR(1), SCR(0), p_bc[:, :]), inc="ev")
                    ev_apply = P.op("vector",
                                    lambda e, c=c: e.tensor_scalar(
                                        actb[:, c * TH:(c + 1) * TH], SCR(1),
                                        gamt[:, c:c + 1], bett[:, c:c + 1],
                                        ALU.mult, ALU.add), inc="ev")
                ev_apply_h[half] = ev_apply
                pbc_sb_free = ("ev", ev_apply)
                bc_free = ("ev", ev_apply)
                perm_mm = {}
                scr5_rd = 0
                pool_rd = {}
                pool_dflt = P.val("ev")
                for c in range(DC):
                    if c % 2 == 0:
                        for kind, base_oc in (("K", 16 + c), ("V", 32 + c)):
                            sl = (c // 2) % 2 if kind == "K" else \
                                ((c // 2) % 2 + 1) % 2
                            wofs = sl * 5632
                            wsem, wval = wslot_free[sl]
                            dsv = 0
                            for kc in range(DC):
                                dsv = P.dma(
                                    "sync",
                                    wbuf[:, wofs + kc * 256:
                                         wofs + kc * 256 + 256],
                                    winT[kc * 128:(kc + 1) * 128,
                                         base_oc * 128:base_oc * 128 + 256],
                                    waits=([(wsem, wval), ("cc", cc_win)]
                                           if kc == 0 else ()),
                                    inc=f"ws{sl}")
                            if kind == "K":
                                dsv_K, wofs_K = dsv, wofs
                            else:
                                dsv_V, wofs_V = dsv, wofs
                    if c < 8:
                        for which, tbl in ((0, rope_cos), (1, rope_sin)):
                            k = (c % 2) * 2 + which
                            wts = [w for w in rope_user[k]]
                            rope_user[k] = []
                            rope_cnt[k] = P.dma(
                                "gpsimd", rope[k],
                                tbl[c * 128:(c + 1) * 128,
                                    half * TH:half * TH + TH],
                                waits=wts, inc=f"rp{k}")
                    bK = p_mm[(c % 2) * 2]
                    bV = p_mm[(c % 2) * 2 + 1]
                    koff = (c % 2) * 256
                    for kind, bb, oc in (("K", bK, 16 + c), ("V", bV, 32 + c)):
                        wofs = wofs_K if kind == "K" else wofs_V
                        dsv = dsv_K if kind == "K" else dsv_V
                        sub = (c % 2) * 128
                        for kc in range(DC):
                            wts = []
                            if kc == 0:
                                wts = [(f"ws{0 if wofs == 0 else 1}", dsv),
                                       ("ev", ev_apply),
                                       bank_free[(c % 2) * 2 +
                                                 (0 if kind == "K" else 1)]]
                            P.op("tensor",
                                 lambda e, kc=kc, wofs=wofs, sub=sub, bb=bb:
                                 e.matmul(bb[:, :],
                                          wbuf[:, wofs + kc * 256 + sub:
                                               wofs + kc * 256 + sub + 128],
                                          actb[:, kc * TH:(kc + 1) * TH],
                                          start=(kc == 0), stop=(kc == DC - 1)),
                                 waits=wts,
                                 inc=("mm" if kc == DC - 1 else None))
                        mm_qkv = P.cnt["mm"]
                        if c % 2 == 1:
                            wslot_free[0 if wofs == 0 else 1] = \
                                ("mm", mm_qkv)
                        ss = SCR(0) if kind == "K" else SCR(1)
                        aev = P.op("scalar",
                                   lambda e, bb=bb, oc=oc, ss=ss:
                                   e.activation(ss, bb[:, :], AF.Identity,
                                                bias=bias_q[:, oc:oc + 1]),
                                   waits=[("mm", mm_qkv),
                                          ("mm", perm_mm.get(
                                              (kind, c - 1), 0)),
                                          ("ev", kraw_rd.get(
                                              (half, kind, c - 1), 0))],
                                   inc="ac")
                        bank_free[(c % 2) * 2 + (0 if kind == "K" else 1)] = \
                            ("ac", aev)
                        sq = sqK if kind == "K" else sqV
                        evsq = P.op("vector",
                                    lambda e, ss=ss, sq=sq: e.tensor_mul(
                                        sq, ss, ss),
                                    waits=[("ac", aev)] +
                                    ([ps_free] if c == 0 else []), inc="ev")
                        pbank = p_s1 if kind == "K" else p_s2
                        P.op("tensor", lambda e, sq=sq, pbank=pbank, c=c:
                             e.matmul(pbank[0:1, :], cOnes16[:], sq,
                                      start=(c == 0), stop=(c == DC - 1)),
                             waits=[("ev", evsq)],
                             inc=("mm" if c == DC - 1 else None))
                        if c == DC - 1:
                            ss_mm = P.cnt["mm"]
                        dst = big[:, (A_KROT if kind == "K" else A_VROT) +
                                  c * TH:(A_KROT if kind == "K" else A_VROT) +
                                  (c + 1) * TH]
                        if c < 8:
                            pp = p_ax if kind == "K" else p_bc
                            ppfree = pax_free if kind == "K" else bc_free
                            pm = P.op("tensor",
                                      lambda e, ss=ss, pp=pp: e.matmul(
                                          pp[:, :], cPerm[:], ss,
                                          start=True, stop=True),
                                      waits=[("ac", aev), ppfree, ds_consts],
                                      inc="mm")
                            perm_mm[(kind, c)] = pm
                            psc = SCR(2) if kind == "K" else SCR(3)
                            apv = P.op("scalar",
                                       lambda e, pp=pp, psc=psc:
                                       e.copy(psc, pp[:, :]),
                                       waits=[("mm", pm),
                                              ("ev", pool_rd.get(
                                                  (kind, c - 1), pool_dflt))],
                                       inc="ac")
                            if kind == "K":
                                pax_free = ("ac", apv)
                            else:
                                bc_free = ("ac", apv)
                            kcos = (c % 2) * 2
                            ksin = kcos + 1
                            evt = P.op("vector",
                                       lambda e, ss=ss, kcos=kcos:
                                       e.tensor_mul(SCR(4), ss, rope[kcos]),
                                       waits=[(f"rp{kcos}", rope_cnt[kcos]),
                                              ("ac", aev)],
                                       inc="ev")
                            plv = P.op("vector",
                                       lambda e, psc=psc, ksin=ksin:
                                       e.tensor_tensor(SCR(5), psc,
                                                       rope[ksin], ALU.mult),
                                       waits=[("ac", apv),
                                              (f"rp{ksin}", rope_cnt[ksin])],
                                       inc="ev")
                            pool_rd[(kind, c)] = plv
                            evo = P.op("vector",
                                       lambda e, dst=dst: e.tensor_add(
                                           dst, SCR(4), SCR(5)), inc="ev")
                            scr5_rd = evo
                            rope_user[kcos].append(("ev", evo))
                            rope_user[ksin].append(("ev", evo))
                            kraw_rd[(half, kind, c)] = evo
                        else:
                            evo = P.op("vector",
                                       lambda e, ss=ss, dst=dst:
                                       e.tensor_copy(dst, ss),
                                       waits=[("ac", aev)], inc="ev")
                            kraw_rd[(half, kind, c)] = evo
                ev_kv = P.val("ev")
                evA = P.op("vector", lambda e: e.tensor_scalar_mul(
                    ROWS[0], p_s1[0:1, :], 1.0),
                    waits=[("mm", ss_mm)], inc="ev")
                evB = P.op("vector", lambda e: e.tensor_mul(
                    ROWS[1], ROWS[0], p_s2[0:1, :]), inc="ev")
                ps_free = ("ev", evB)
                aqw = P.op("scalar", lambda e: e.activation(
                    ROWS[2], ROWS[1], AF.Sqrt),
                    waits=[("ev", evB)], inc="ac")
                evw = P.op("vector", lambda e: e.reciprocal(ROWS[2], ROWS[2]),
                           waits=[("ac", aqw)], inc="ev")
                a0 = bcast_row(ROWS[2], 0, evw)
                at = atile if half == 0 else atile2
                evred = 0
                for c in range(DC):
                    kk = big[:, A_KROT + c * TH:A_KROT + (c + 1) * TH]
                    vv = big[:, A_VROT + c * TH:A_VROT + (c + 1) * TH]
                    P.op("vector", lambda e, kk=kk, vv=vv: e.tensor_mul(
                        SCR(2), kk, vv),
                        waits=([("ac", a0)] if c == 0 else ()), inc="ev")
                    P.op("vector", lambda e: e.tensor_mul(
                        SCR(2), SCR(2), bcast[:, 0:TH]), inc="ev")
                    evred = P.op("vector", lambda e, at=at, c=c: e.reduce_sum(
                        at[:, c:c + 1], SCR(2), axis=AX), inc="ev")
                pbc_sb_free = ("ev", evred)
                for c in range(DC):
                    if c % 2 == 0:
                        sl = (c // 2) % 2
                        wofs = sl * 5632
                        wsem, wval = wslot_free[sl]
                        dsv = 0
                        for kc in range(DC):
                            dsv = P.dma(
                                "sync",
                                wbuf[:, wofs + kc * 256:wofs + kc * 256 + 256],
                                winT[kc * 128:(kc + 1) * 128,
                                     c * 128:c * 128 + 256],
                                waits=([(wsem, wval)] if kc == 0 else ()),
                                inc=f"ws{sl}")
                    sl = (c // 2) % 2
                    wofs = sl * 5632
                    sub = (c % 2) * 128
                    bb = p_mm[c % 4]
                    for kc in range(DC):
                        wts = []
                        if kc == 0:
                            wts = [(f"ws{sl}", dsv), bank_free[c % 4]]
                        mq = P.op("tensor",
                                  lambda e, kc=kc, wofs=wofs, sub=sub, bb=bb:
                                  e.matmul(bb[:, :],
                                           wbuf[:, wofs + kc * 256 + sub:
                                                wofs + kc * 256 + sub + 128],
                                           actb[:, kc * TH:(kc + 1) * TH],
                                           start=(kc == 0),
                                           stop=(kc == DC - 1)),
                                  waits=wts,
                                  inc=("mm" if kc == DC - 1 else None))
                    if c % 2 == 1:
                        wslot_free[sl] = ("mm", mq)
                    qd = big[:, A_QY + c * T + half * TH:
                             A_QY + c * T + half * TH + TH]
                    aq_ = P.op("scalar", lambda e, bb=bb, c=c, qd=qd:
                               e.activation(qd, bb[:, :], AF.Identity,
                                            bias=bias_q[:, c:c + 1]),
                               waits=[("mm", mq)], inc="ac")
                    bank_free[c % 4] = ("ac", aq_)
                    q_ac[(half, c)] = aq_
                if half == 0 and _os.environ.get("ATTN_STOP", "0") == "1":
                    evx = P.val("ev")
                    acx = P.val("ac")
                    if "xn" in tap:
                        P.dma("gpsimd", tap["xn"][:, :], actb[:, :],
                              waits=[("ev", evx), ("ac", acx)])
                    if "kr" in tap:
                        P.dma("gpsimd", tap["kr"][:, :],
                              big[:, A_KROT:A_KROT + 8192],
                              waits=[("ev", evx), ("ac", acx)])
                    if "vr" in tap:
                        P.dma("gpsimd", tap["vr"][:, :],
                              big[:, A_VROT:A_VROT + 8192],
                              waits=[("ev", evx), ("ac", acx)])
                    if "qy" in tap:
                        P.dma("gpsimd", tap["qy"][:, :],
                              big[:, A_QY:A_QY + 16384],
                              waits=[("ev", evx), ("ac", acx)])
                    if "at" in tap:
                        P.dma("gpsimd", tap["at"][:, :], atile[:, 0:16],
                              waits=[("ev", evx), ("ac", acx)])
                    return ("ev", evx)
                if half == 1 and _os.environ.get("ATTN_STOP2", "0") == "1":
                    evx = P.val("ev")
                    acx = P.val("ac")
                    if "kr" in tap:
                        P.dma("gpsimd", tap["kr"][:, :],
                              big[:, A_KROT:A_KROT + 8192],
                              waits=[("ev", evx), ("ac", acx)])
                    if "vr" in tap:
                        P.dma("gpsimd", tap["vr"][:, :],
                              big[:, A_VROT:A_VROT + 8192],
                              waits=[("ev", evx), ("ac", acx)])
                    if "at" in tap:
                        P.dma("gpsimd", tap["at"][:, :], atile2[:, :],
                              waits=[("ev", evx), ("ac", acx)])
                    if "xn" in tap:
                        P.dma("gpsimd", tap["xn"][:, 0:32].bitcast(F32),
                              atile[:, 0:16],
                              waits=[("ev", evx), ("ac", acx)])
                    return ("ev", evx)
                if half == 1:
                    for _ in range(2):
                        P.op("vector", lambda e: e.tensor_scalar_mul(
                            SCR(2), SCR(2), 1.0), inc="ev")
                    ev_Adone = P.val("ev")
            A_LOCAL = _os.environ.get("A_LOCAL", "0") == "1"
            if A_LOCAL:
                P.op("vector", lambda e: e.tensor_add(
                    SCR(5)[:, 128:144], atile[:, 0:16], atile2[:, 0:16]),
                    waits=[("ev", ev_Adone)], inc="ev")
                evAp = P.op("vector", lambda e: e.tensor_scalar(
                    SCR(5)[:, 160:176], SCR(5)[:, 128:144], 0.5, 0.5,
                    ALU.mult, ALU.add), inc="ev")
            else:
                P.op("vector", lambda e: e.tensor_add(
                    SCR(5)[:, 0:16], atile2[:, 0:16], atile[:, 0:16]),
                    waits=[("ev", ev_Adone)], inc="ev")
                P.op("vector", lambda e: e.tensor_add(
                    SCR(5)[:, 64:80], atile2[:, 0:16], atile[:, 0:16]),
                    inc="ev")
                for _ in range(2):
                    ev_Adone2 = P.op("vector", lambda e: e.tensor_scalar_mul(
                        SCR(2), SCR(2), 1.0), inc="ev")
                if "vr" in tap:
                    P.dma("gpsimd", tap["vr"][:, 0:256],
                          SCR(5)[:, 0:128].bitcast(BF16),
                          waits=[("ev", ev_Adone2)])
                P.dma("gpsimd", arA_in, SCR(5)[:, 0:128],
                      waits=[("ev", ev_Adone2)], inc="arA")
                ccA = P.op("gpsimd", lambda e: e.collective_compute(
                    "AllGather", ALU.bypass, replica_groups=RG8,
                    ins=[arA_in.opt()], outs=[arA_out.opt()]),
                    waits=[("arA", P.val("arA"))], inc="cc")
                plA = P.op("gpsimd", lambda e: e.tensor_copy(
                    big[:, A_RP:A_RP + 64], big[:, A_RP:A_RP + 64]),
                    waits=[("cc", ccA)], inc="pl")
                P.dma("gpsimd",
                      scr[:, 3 * TH:5 * TH].rearrange("p (r c) -> p r c",
                                                      c=128),
                      arA_out.rearrange("(r p) c -> p r c", p=128),
                      waits=[("pl", plA)], inc="arA")
                arA_t = P.val("arA")
                for _ in range(2):
                    P.op("vector", lambda e: e.tensor_scalar_mul(
                        SCR(2), SCR(2), 1.0), waits=[("arA", arA_t)],
                        inc="ev")
                B = 3 * TH
                for i in range(4):
                    P.op("vector", lambda e, i=i: e.tensor_add(
                        SCR(5)[:, 176 + 16 * i:192 + 16 * i],
                        scr[:, B + 256 * i:B + 256 * i + 16],
                        scr[:, B + 256 * i + 128:B + 256 * i + 144]),
                        inc="ev")
                P.op("vector", lambda e: e.tensor_add(
                    SCR(5)[:, 128:144], SCR(5)[:, 176:192],
                    SCR(5)[:, 192:208]), inc="ev")
                P.op("vector", lambda e: e.tensor_add(
                    SCR(5)[:, 144:160], SCR(5)[:, 208:224],
                    SCR(5)[:, 224:240]), inc="ev")
                for _ in range(2):
                    P.op("vector", lambda e: e.tensor_scalar_mul(
                        SCR(2), SCR(2), 1.0), inc="ev")
                P.op("vector", lambda e: e.tensor_add(
                    SCR(5)[:, 240:256], SCR(5)[:, 128:144],
                    SCR(5)[:, 144:160]), inc="ev")
                for _ in range(2):
                    P.op("vector", lambda e: e.tensor_scalar_mul(
                        SCR(2), SCR(2), 1.0), inc="ev")
                evAp = P.op("vector", lambda e: e.tensor_scalar(
                    SCR(5)[:, 160:176], SCR(5)[:, 240:256], 0.5, 0.5,
                    ALU.mult, ALU.add), inc="ev")
                if "xn" in tap:
                    P.dma("gpsimd", tap["xn"][:, 0:2048],
                          scr[:, 3 * TH:5 * TH].bitcast(BF16),
                          waits=[("ev", evAp)])
            if "at" in tap and _os.environ.get("ATTN_STOP", "0") != "1":
                evsl = P.op("vector", lambda e: e.tensor_copy(
                    atile[:, 0:16], SCR(5)[:, 160:176]),
                    waits=[("ev", evAp)], inc="ev")
                for _ in range(2):
                    evsl = P.op("vector", lambda e: e.tensor_scalar_mul(
                        SCR(2), SCR(2), 1.0), inc="ev")
                P.dma("gpsimd", tap["at"][:, :], atile[:, 0:16],
                      waits=[("ev", evsl)])
            evY = {}
            for half in range(2):
                for c in range(DC):
                    qd = big[:, A_QY + c * T + half * TH:
                             A_QY + c * T + half * TH + TH]
                    evY[(half, c)] = P.op(
                        "vector", lambda e, qd=qd, c=c: e.tensor_scalar(
                            qd, qd, SCR(5)[:, 160 + c:161 + c], None,
                            ALU.mult),
                        waits=([("ev", evAp)] if (half, c) == (0, 0) else []) +
                        [("ac", q_ac[(half, c)])],
                        inc="ev")
                for _ in range(2):
                    evY[(half, DC - 1)] = P.op(
                        "vector", lambda e: e.tensor_scalar_mul(
                            SCR(2), SCR(2), 1.0), inc="ev")
            if "qy" in tap and _os.environ.get("ATTN_STOP", "0") != "1":
                P.dma("gpsimd", tap["qy"][:, :], big[:, A_QY:A_QY + 16384],
                      waits=[("ev", evY[(1, DC - 1)])])
            if "woT" in tap:
                P.dma("gpsimd", tap["woT"][:, :], woT[:, :],
                      waits=[("cc", cc_wo)])
            last_ev = 0
            for half in range(2):
                for c in range(DC):
                    if c % 2 == 0:
                        sl = (c // 2) % 2
                        wofs = sl * 5632
                        wsem, wval = wslot_free[sl]
                        dsv = 0
                        for kc in range(DC):
                            dsv = P.dma(
                                "sync",
                                wbuf[:, wofs + kc * 256:wofs + kc * 256 + 256],
                                woT[kc * 128:(kc + 1) * 128,
                                    c * 128:c * 128 + 256],
                                waits=([(wsem, wval), ("cc", cc_wo)]
                                       if kc == 0 else ()),
                                inc=f"ws{sl}")
                    sl = (c // 2) % 2
                    wofs = sl * 5632
                    sub = (c % 2) * 128
                    bb = p_mm[c % 4]
                    for kc in range(DC):
                        wts = []
                        if kc == 0:
                            wts = [(f"ws{sl}", dsv), bank_free[c % 4],
                                   ("ev", evY[(half, DC - 1)])]
                        mo = P.op("tensor",
                                  lambda e, kc=kc, wofs=wofs, sub=sub, bb=bb,
                                  half=half:
                                  e.matmul(bb[:, :],
                                           wbuf[:, wofs + kc * 256 + sub:
                                                wofs + kc * 256 + sub + 128],
                                           big[:, A_QY + kc * T + half * TH:
                                               A_QY + kc * T + half * TH + TH],
                                           start=(kc == 0),
                                           stop=(kc == DC - 1)),
                                  waits=wts,
                                  inc=("mm" if kc == DC - 1 else None))
                    if c % 2 == 1:
                        wslot_free[sl] = ("mm", mo)
                    ao = P.op("scalar", lambda e, bb=bb, c=c:
                              e.activation(SCR(4), bb[:, :], AF.Identity,
                                           bias=bias_o[:, c:c + 1]),
                              waits=[("mm", mo), ("ev", last_ev)],
                              inc="ac")
                    bank_free[c % 4] = ("ac", ao)
                    xsl = xfm[:, c * T + half * TH:c * T + half * TH + TH]
                    last_ev = P.op("vector", lambda e, xsl=xsl: e.tensor_add(
                        xsl, xsl, SCR(4)), waits=[("ac", ao)], inc="ev")
            return ("ev", last_ev)

        kraw_rd = {}
        if _os.environ.get("SKIP_ATTN", "0") == "1":
            attn_done = ffn1_done
        else:
            attn_done = attn_call(ffn1_done)
        if "attn" in tap:
            xdump(tap["attn"], attn_done)

        if _os.environ.get("SKIP_FFN2", "0") == "1":
            ffn2_done = attn_done
        else:
            ffn2_done = ffn_call(attn_done, "ffn2")

        pend = []
        for tc in range(8):
            for dc in range(DC):
                i = tc * DC + dc
                b = i % 4
                sl = i % 3
                stile = scr[:, (2 + sl) * TH:(2 + sl) * TH + 128]
                semv = bank_free[b]
                t = P.op("tensor", lambda e, tc=tc, dc=dc, b=b:
                         e.transpose(p_mm[b][:, 0:128],
                                     xfm[:, dc * T + tc * 128:
                                         dc * T + tc * 128 + 128],
                                     cI32[:]),
                         waits=[ffn2_done, semv], inc="mm")
                a = P.op("scalar",
                         lambda e, b=b, stile=stile:
                         e.copy(stile, p_mm[b][:, 0:128]),
                         waits=[("mm", t), (f"to{sl}", P.val(f"to{sl}"))],
                         inc="ac")
                bank_free[b] = ("ac", a)
                pend.append((tc, dc, sl, stile))
                while len(pend) > 2:
                    q = pend.pop(0)
                    P.dma("sync",
                          y_out[q[0] * 128:(q[0] + 1) * 128,
                                q[1] * 128:(q[1] + 1) * 128],
                          q[3], waits=[("ac", a)], inc=f"to{q[2]}")
        for _ in range(2):
            a = P.op("scalar", lambda e: e.copy(bcast[0:1, 0:4],
                                                cV[0:1, 0:4]), inc="ac")
        while pend:
            q = pend.pop(0)
            P.dma("sync",
                  y_out[q[0] * 128:(q[0] + 1) * 128,
                        q[1] * 128:(q[1] + 1) * 128],
                  q[3], waits=[("ac", a)], inc=f"to{q[2]}")

        P.emit(block)
        fin_vals = [(k, P.val(k)) for k in list(P.cnt.keys())]

        def fin(eng):
            for k, v in fin_vals:
                if v:
                    eng.wait_ge(P.sems[k], v)
        block.sync(fin)
    return nc


def host_inputs(L0_attn_X, L0_attn_W_in, L0_attn_b_in, L0_attn_W_out,
                L0_attn_b_out, L0_attn_gamma, L0_attn_beta,
                L0_ffn_gate_up_packed, L0_ffn_gate_up_scales,
                L0_ffn_down_packed, L0_ffn_down_scales):
    X = np.ascontiguousarray(np.asarray(L0_attn_X, np.float32).reshape(8192, D))
    W_in = np.asarray(L0_attn_W_in, np.float32)
    W_out = np.asarray(L0_attn_W_out, np.float32)
    gupB = np.ascontiguousarray(np.asarray(L0_ffn_gate_up_packed, np.int32)) \
        .view(np.uint8)
    dwnB = np.ascontiguousarray(np.asarray(L0_ffn_down_packed, np.int32)) \
        .view(np.uint8)
    gsc = np.asarray(L0_ffn_gate_up_scales)
    dsc = np.asarray(L0_ffn_down_scales)

    ident = np.eye(128, dtype=np.float32)
    perm = np.zeros((128, 128), np.float32)
    for jj in range(64):
        perm[2 * jj + 1, 2 * jj] = -1.0
        perm[2 * jj, 2 * jj + 1] = 1.0
    onesrow = np.ones((1, 128), np.float32)
    ones16 = np.ones((128, 1), BF)
    cvals = np.zeros((128, 4), np.float32)
    cvals[:, 0] = -4.0
    cvals[:, 1] = -6.0
    cvals[:, 2] = EPS_LN
    cvals[:, 3] = EPS_RMS

    inv = 1.0 / (10000.0 ** (np.arange(0, 1024, 2, dtype=np.float64) / 1024.0))
    in_maps = []
    for r in range(8):
        pos = np.arange(r * T, (r + 1) * T, dtype=np.float64)
        fr = np.outer(pos, inv)
        cos = np.repeat(np.cos(fr), 2, 1).T.astype(np.float32)
        sin = np.repeat(np.sin(fr), 2, 1).T.astype(np.float32)
        gup_sl = np.ascontiguousarray(np.concatenate(
            [gupB[:, j * 256 + r * 32: j * 256 + r * 32 + 32]
             for j in range(4)], axis=1))
        gsc_sl = np.ascontiguousarray(
            gsc[[4 * j + r // 2 for j in range(4)], :]).reshape(352, 128)
        dwn_sl = np.ascontiguousarray(dwnB[:, r * 352:(r + 1) * 352])
        grow = (704 * r + 2 * np.arange(352)) // 128
        dsc_exp = np.ascontiguousarray(dsc[grow, :].T)
        in_maps.append({
            "x_in": np.ascontiguousarray(X[r * T:(r + 1) * T]),
            "win_sl": np.ascontiguousarray(W_in[r * 256:(r + 1) * 256]),
            "wout_sl": np.ascontiguousarray(W_out[r * 256:(r + 1) * 256]),
            "b_in": np.asarray(L0_attn_b_in, np.float32).reshape(48, 128),
            "b_out": np.asarray(L0_attn_b_out, np.float32).reshape(16, 128),
            "gam_in": np.asarray(L0_attn_gamma, np.float32).reshape(16, 128),
            "bet_in": np.asarray(L0_attn_beta, np.float32).reshape(16, 128),
            "gup_by": gup_sl,
            "gup_sc": gsc_sl,
            "dwn_by": dwn_sl,
            "dwn_sc": dsc_exp,
            "rope_cos": np.ascontiguousarray(cos),
            "rope_sin": np.ascontiguousarray(sin),
            "ident_f32": ident,
            "perm_f32": perm,
            "onesrow_f32": onesrow,
            "ones_bf16": ones16,
            "cvals": cvals,
        })
    return in_maps


def kernel(**inputs):
    in_maps = host_inputs(**inputs)
    nc = build_program()
    res = run_bass_kernel_spmd(nc, in_maps, core_ids=list(range(8)))
    out = np.concatenate([res.results[r]["y_out"] for r in range(8)], axis=0)
    return out.reshape(1, 8192, D)


# version 03 (final)
# speedup vs baseline: 1.0512x; 1.0512x over previous
import math
import numpy as np
import ml_dtypes
from contextlib import ExitStack
import concourse.bass as bass
import concourse.mybir as mybir
from concourse.bass_utils import run_bass_kernel_spmd

F32 = mybir.dt.float32
BF16 = mybir.dt.bfloat16
F16 = mybir.dt.float16
U8 = mybir.dt.uint8
AF = mybir.ActivationFunctionType
ALU = mybir.AluOpType
AX = mybir.AxisListType.X
RG8 = [[0, 1, 2, 3, 4, 5, 6, 7]]
BF = ml_dtypes.bfloat16

T = 1024
TH = 512
D = 2048
DC = 16
I2 = 11264
HC = 44
EPS_LN = 1e-5
EPS_RMS = 1e-6

BIG_COLS = 36864
A_H = 0
A_Y = 22528
A_SQ = 30720
A_KROT = 0
A_VROT = 8192
A_QY = 16384
A_RP = 32768
A_GBY = 0
A_DBY = 5632
A_DQ = 8448
A_STG = 22528


class Prog:
    ENGINES = ("sync", "vector", "scalar", "tensor", "gpsimd")

    def __init__(self, nc, ctx):
        self.nc = nc
        self.ctx = ctx
        self.q = {e: [] for e in self.ENGINES}
        self.sems = {}
        self.cnt = {}

    def sem(self, name):
        if name not in self.sems:
            self.sems[name] = self.ctx.enter_context(self.nc.semaphore(name))
            self.cnt[name] = 0
        return self.sems[name]

    def val(self, name):
        self.sem(name)
        return self.cnt[name]

    scope = "init"

    def op(self, eng, fn, waits=(), inc=None, n=1):
        waits = tuple((s, v) for (s, v) in waits if v and v > 0)
        for s, _ in waits:
            self.sem(s)
        tgt = None
        if inc is not None:
            self.sem(inc)
            self.cnt[inc] += n
            tgt = self.cnt[inc]
        self.q[eng].append((fn, waits, inc, n, Prog.scope))
        return tgt

    def dma(self, eng, dst, src, waits=(), inc=None):
        if inc is None:
            inc = "ds" if eng == "sync" else "dg"
        return self.op(eng, lambda e, d=dst, s=src: e.dma_start(out=d, in_=s),
                       waits, inc, 16)

    def emit(self, block):
        for e in self.ENGINES:
            items = self.q[e]
            if not items:
                continue

            def body(eng, items=items):
                cur_scope = None
                cur_id = None
                for fn, waits, inc, n, sc in items:
                    if sc != cur_scope:
                        if cur_scope is not None:
                            self.nc.leave_named_scope(cur_scope, cur_id,
                                                      False)
                        cur_id, _ = self.nc.enter_named_scope(sc, False)
                        cur_scope = sc
                    for s, v in waits:
                        eng.wait_ge(self.sems[s], v)
                    inst = fn(eng)
                    if inc is not None:
                        inst.then_inc(self.sems[inc], n)
                if cur_scope is not None:
                    self.nc.leave_named_scope(cur_scope, cur_id, False)
            getattr(block, e)(body)


def build_program(taps=()):
    taps = set(taps)
    nc = bass.Bass("TRN2", target_bir_lowering=False)

    def din(name, shape, dt):
        return nc.dram_tensor(name, shape, dt, kind="ExternalInput").ap()

    x_in = din("x_in", [T, D], F32)
    win_sl = din("win_sl", [256, 6144], F32)
    wout_sl = din("wout_sl", [256, 2048], F32)
    b_in = din("b_in", [48, 128], F32)
    b_out = din("b_out", [16, 128], F32)
    gam_in = din("gam_in", [16, 128], F32)
    bet_in = din("bet_in", [16, 128], F32)
    gup_by = din("gup_by", [I2, 128], U8)
    gup_sc = din("gup_sc", [352, 128], F16)
    dwn_by = din("dwn_by", [D, 352], U8)
    dwn_sc = din("dwn_sc", [D, 352], F16)
    rope_cos = din("rope_cos", [1024, T], F32)
    rope_sin = din("rope_sin", [1024, T], F32)
    ident_f32 = din("ident_f32", [128, 128], F32)
    perm_f32 = din("perm_f32", [128, 128], F32)
    onesrow_f32 = din("onesrow_f32", [1, 128], F32)
    ones_bf16 = din("ones_bf16", [128, 1], BF16)
    cvals = din("cvals", [128, 4], F32)

    y_out = nc.dram_tensor("y_out", [T, D], F32, kind="ExternalOutput").ap()
    tap = {}
    tapspec = {"wguT": ([D, I2], BF16), "wdT": ([5632, D], BF16),
               "wdP": ([704, D], BF16), "ln1h0": ([128, DC * TH], BF16),
               "hh0": ([128, HC * TH], BF16), "yh0": ([128, DC * TH], BF16),
               "xn": ([128, DC * TH], BF16), "kr": ([128, DC * TH], BF16),
               "vr": ([128, DC * TH], BF16), "qy": ([128, DC * T], BF16),
               "at": ([128, 16], F32), "ari": ([128, 80], F32),
               "aro": ([128, 80], F32),
               "winT": ([D, 6144], BF16), "woT": ([D, D], BF16),
               "ffn1": ([T, D], F32), "attn": ([T, D], F32)}
    for tn in taps:
        shp, dtp = tapspec[tn]
        tap[tn] = nc.dram_tensor("tap_" + tn, shp, dtp,
                                 kind="ExternalOutput").ap()

    def idram(name, shape, dt):
        return nc.dram_tensor(name, shape, dt, kind="Internal").ap()

    wgu_piece = idram("wgu_piece", [256, I2], BF16)
    wguT = idram("wguT", [D, I2], BF16)
    wd_piece = idram("wd_piece", [704, D], BF16)
    wdT = idram("wdT", [5632, D], BF16)
    win_piece = idram("win_piece", [256, 6144], BF16)
    winT = idram("winT", [D, 6144], BF16)
    wo_piece = idram("wo_piece", [256, D], BF16)
    woT = idram("woT", [D, D], BF16)
    arA_in = idram("arA_in", [128, 128], F32)
    arA_out = idram("arA_out", [1024, 128], F32)

    with ExitStack() as ctx:
        def sbt(name, shape, dt):
            return ctx.enter_context(nc.sbuf_tensor(name, shape, dt))

        def pst(name, shape, dt=F32):
            return ctx.enter_context(nc.psum_tensor(name, shape, dt))

        xfm = sbt("xfm", [128, DC * T], F32)
        big = sbt("big", [128, BIG_COLS], BF16)
        actb = sbt("actb", [128, DC * TH], BF16)
        wbuf = sbt("wbuf", [128, 11264], BF16)
        scr = sbt("scr", [128, 10 * TH], F32)
        bcast = sbt("bcast", [128, 2 * TH], F32)
        cI32 = sbt("cI32", [128, 128], F32)
        cPerm = sbt("cPerm", [128, 128], F32)
        cOnesR = sbt("cOnesR", [1, 128], F32)
        cOnes16 = sbt("cOnes16", [128, 1], BF16)
        cV = sbt("cV", [128, 4], F32)
        bias_q = sbt("bias_q", [128, 48], F32)
        bias_o = sbt("bias_o", [128, 16], F32)
        gamt = sbt("gamt", [128, 16], F32)
        bett = sbt("bett", [128, 16], F32)
        atile = sbt("atile", [128, 32], F32)
        atile2 = sbt("atile2", [128, 32], F32)
        aio = sbt("aio", [128, 80], F32)
        scT16 = sbt("scT16", [128, 352], F16)
        scTg = sbt("scTg", [128, 352], F32)


        p_mm = [pst(f"pmm{i}", [128, 512]) for i in range(4)]
        p_s1 = pst("ps1", [128, 512])
        p_s2 = pst("ps2", [128, 512])
        p_bc = pst("pbc", [128, 512])
        p_ax = pst("pax", [128, 512])

        P = Prog(nc, ctx)
        block = ctx.enter_context(nc.Block())

        SCR = lambda i: scr[:, i * TH:(i + 1) * TH]
        ROWS = [scr[0:1, (6 + i) * TH:(6 + i) * TH + TH] for i in range(3)]
        xio = actb[:].bitcast(F32)

        bank_free = {i: ("ev", 0) for i in range(4)}
        mm_last = {}

        P.dma("sync", cI32[:], ident_f32, inc="ldc")
        P.dma("sync", cPerm[:], perm_f32, inc="ldc")
        P.dma("sync", cOnesR[:], onesrow_f32, inc="ldc")
        P.dma("sync", cOnes16[:], ones_bf16, inc="ldc")
        P.dma("sync", cV[:], cvals, inc="ldc")
        ds_consts = ("ldc", P.val("ldc"))

        P.op("sync", lambda e: e.dma_start(out=scT16[:], in_=gup_sc,
                                           transpose=True), inc="ldc", n=16)
        ds_sc = P.val("ldc")
        gby = big[:, A_GBY:A_GBY + 5632].bitcast(U8)
        dby = big[:, A_DBY:A_DBY + 2816].bitcast(U8)
        P.dma("sync", gby.rearrange("p (oc b) -> p oc b", b=128),
              gup_by.rearrange("(oc p) b -> oc p b", p=128).transpose([1, 0, 2]),
              inc="ldb")
        P.dma("sync", dby.rearrange("p (oc b) -> p oc b", b=352),
              dwn_by.rearrange("(oc p) b -> oc p b", p=128).transpose([1, 0, 2]),
              inc="ldb")
        ds_bytes = ("ldb", P.val("ldb"))

        P.op("vector", lambda e: e.memset(aio[:, 16:80], 0.0), inc="ev")
        P.op("vector", lambda e: e.memset(atile[:, :], 0.0), inc="ev")
        P.op("vector", lambda e: e.memset(atile2[:, :], 0.0), inc="ev")
        ev_scg = P.op("vector", lambda e: e.tensor_copy(scTg[:], scT16[:]),
                      waits=[("ldc", ds_sc)], inc="ev")


        P.dma("gpsimd", win_piece, win_sl)
        P.dma("gpsimd", wo_piece, wout_sl)
        dg_wio = P.val("dg")

        Prog.scope = "xload"
        xT_last = {}
        for tc in range(8):
            w = [("mm", xT_last.get(tc - 2, 0))]
            dsx = P.dma("sync", xio[:, (tc % 2) * D:(tc % 2) * D + D],
                        x_in[tc * 128:(tc + 1) * 128, :], waits=w,
                        inc=f"xs{tc % 2}")
            for dc in range(DC):
                b = (tc * DC + dc) % 4
                sem, v = bank_free[b]
                t = P.op("tensor",
                         lambda e, tc=tc, dc=dc, b=b:
                         e.transpose(p_mm[b][:, 0:128],
                                     xio[:, (tc % 2) * D + dc * 128:
                                         (tc % 2) * D + dc * 128 + 128],
                                     cI32[:]),
                         waits=[(f"xs{tc % 2}", dsx), (sem, v), ds_consts],
                         inc="mm")
                a = P.op("scalar",
                         lambda e, tc=tc, dc=dc, b=b:
                         e.copy(xfm[:, dc * T + tc * 128:dc * T + tc * 128 + 128],
                                p_mm[b][:, 0:128]),
                         waits=[("mm", t)], inc="ac")
                bank_free[b] = ("ac", a)
            xT_last[tc] = t
        ac_xdone = P.val("ac")

        btr_prev_mm = 0
        btr_prev_ac = 0

        def load_tr(src_dram, nrows, dst_sbuf):
            nonlocal btr_prev_mm, btr_prev_ac
            st = scr[0:nrows, 8 * TH:8 * TH + 128]
            dsv = P.dma("sync", st, src_dram, waits=[("mm", btr_prev_mm)],
                        inc="ldc")
            t = P.op("tensor", lambda e, st=st, nrows=nrows:
                     e.transpose(p_ax[0:128, 0:nrows], st,
                                 cI32[0:nrows, 0:nrows]),
                     waits=[("ldc", dsv), ("ac", btr_prev_ac),
                            ds_consts],
                     inc="mm")
            btr_prev_mm = t
            a = P.op("scalar", lambda e, dst_sbuf=dst_sbuf, nrows=nrows:
                     e.copy(dst_sbuf, p_ax[:, 0:nrows]),
                     waits=[("mm", t)], inc="ac")
            btr_prev_ac = a
            return a

        load_tr(b_in, 48, bias_q[:])
        load_tr(b_out, 16, bias_o[:])
        load_tr(gam_in, 16, gamt[:])
        load_tr(bet_in, 16, bett[:])
        mm_bias_done = P.val("mm")
        pax_free0 = ("ac", btr_prev_ac)

        dq = [big[:, A_DQ + i * 2816:A_DQ + (i + 1) * 2816] for i in range(5)]
        wv32 = scr[:, 0:2816]
        stg = big[:, A_STG:A_STG + I2]
        gather_done = {}
        pax_free = pax_free0
        pbc_free = ("ac", 0)

        def decode_batch(src_u8_flat, n, scale3, out_f32, pre=()):
            w0 = list(pre)
            e1 = P.op("vector", lambda e: e.tensor_copy(dq[0][:, 0:n],
                                                        src_u8_flat),
                      waits=w0, inc="ev")
            P.op("vector", lambda e: e.tensor_single_scalar(
                dq[1][:, 0:n], dq[0][:, 0:n], 8.0, ALU.is_ge), inc="ev")
            P.op("vector", lambda e: e.scalar_tensor_tensor(
                dq[2][:, 0:n], dq[1][:, 0:n], -8.0, dq[0][:, 0:n],
                ALU.mult, ALU.add), inc="ev")
            ev_m = P.cnt["ev"]
            a1 = P.op("scalar", lambda e: e.activation(
                dq[3][:, 0:n], dq[2][:, 0:n], AF.Relu, bias=cV[:, 0:1]),
                waits=[("ev", ev_m), ds_consts], inc="ac")
            a2 = P.op("scalar", lambda e: e.activation(
                dq[4][:, 0:n], dq[2][:, 0:n], AF.Relu, bias=cV[:, 1:2]),
                inc="ac")
            P.op("vector", lambda e: e.scalar_tensor_tensor(
                dq[0][:, 0:n], dq[2][:, 0:n], 0.5, dq[4][:, 0:n],
                ALU.mult, ALU.add), waits=[("ac", a2)], inc="ev")
            P.op("vector", lambda e: e.scalar_tensor_tensor(
                dq[2][:, 0:n], dq[3][:, 0:n], 0.5, dq[0][:, 0:n],
                ALU.mult, ALU.add), waits=[("ac", a1)], inc="ev")
            P.op("vector", lambda e: e.tensor_mul(
                dq[0][:, 0:n], dq[1][:, 0:n], dq[2][:, 0:n]), inc="ev")
            P.op("vector", lambda e: e.scalar_tensor_tensor(
                dq[1][:, 0:n], dq[0][:, 0:n], -2.0, dq[2][:, 0:n],
                ALU.mult, ALU.add), inc="ev")
            sc_ap, out3 = scale3
            in3 = dq[1][:, 0:n].rearrange("p (oc b) -> p oc b",
                                          b=sc_ap.shape[2])
            evs = P.op("vector", lambda e: e.tensor_tensor(
                out3, in3, sc_ap, ALU.mult), inc="ev")
            return evs

        Prog.scope = "deq_wgu"
        prev_tr = 0
        prev_dma_par = {0: 0, 1: 0}
        for j in range(4):
            for par in range(2):
                src = gby.rearrange("p (oc b) -> p oc b", b=128)[
                    :, :, j * 32:(j + 1) * 32]
                nib_u8 = dq[4][:, 0:1408].bitcast(U8)
                nib3 = nib_u8.rearrange("p (oc b) -> p oc b", b=32)
                if par == 0:
                    e0 = P.op("vector", lambda e, src=src, nib3=nib3:
                              e.tensor_single_scalar(nib3, src, 15,
                                                     ALU.bitwise_and),
                              waits=[ds_bytes], inc="ev")
                else:
                    e0 = P.op("vector", lambda e, src=src, nib3=nib3:
                              e.tensor_single_scalar(nib3, src, 4,
                                                     ALU.logical_shift_right),
                              waits=[ds_bytes], inc="ev")
                sc_b = scTg[:, j * 88:(j + 1) * 88].unsqueeze(2).broadcast_to([128, 88, 32])
                wv3 = wv32.rearrange("p (oc b) -> p oc b", b=32)
                evs = decode_batch(nib_u8, 2816, (sc_b, wv3), wv32,
                                   pre=[("ev", e0), ("ev", ev_scg),
                                        ("mm", mm_bias_done),
                                        ("mm", prev_tr)])
                for g4 in range(22):
                    pb = p_ax if g4 % 2 == 0 else p_bc
                    pfree = pax_free if g4 % 2 == 0 else pbc_free
                    for oi in range(4):
                        oc = g4 * 4 + oi
                        tmm = P.op("tensor",
                                   lambda e, oc=oc, oi=oi, pb=pb:
                                   e.transpose(pb[0:32, oi * 128:(oi + 1) * 128],
                                               wv32[:, oc * 32:(oc + 1) * 32],
                                               cI32[:]),
                                   waits=([("ev", evs), pfree]
                                          if oi == 0 else ()),
                                   inc=("mm" if oi == 3 else None))
                    oc0 = g4 * 4
                    c0 = oc0 if oc0 < 44 else oc0 - 44
                    off = 0 if oc0 < 44 else 128
                    dst = stg[par * 32:par * 32 + 32, :].rearrange(
                        "p (c u2) -> p c u2", u2=256)[:, c0:c0 + 4, off:off + 128]
                    srcp = pb[0:32, 0:512].rearrange("p (c u) -> p c u", u=128)
                    aev = P.op("scalar",
                               lambda e, dst=dst, srcp=srcp: e.copy(dst, srcp),
                               waits=[("mm", tmm),
                                      (f"sg{par}", prev_dma_par[par])],
                               inc="ac")
                    if g4 % 2 == 0:
                        pax_free = ("ac", aev)
                    else:
                        pbc_free = ("ac", aev)
                prev_tr = tmm
                P.op("scalar", lambda e: e.copy(bcast[0:1, 0:4],
                                                cV[0:1, 0:4]), inc="ac")
                ac_par = P.val("ac")
                prev_dma_par[par] = P.dma(
                    "gpsimd", wgu_piece[j * 64 + par:(j + 1) * 64:2, :],
                    stg[par * 32:par * 32 + 32, :],
                    waits=[("ac", ac_par)], inc=f"sg{par}")
            ccj = P.op("gpsimd", lambda e, j=j: e.collective_compute(
                "AllGather", ALU.bypass, replica_groups=RG8,
                ins=[wgu_piece[j * 64:(j + 1) * 64, :].opt()],
                outs=[wguT[j * 512:(j + 1) * 512, :].opt()]),
                waits=[("sg0", P.val("sg0")), ("sg1", P.val("sg1"))],
                inc="cc")
            gather_done[("wgu", j)] = ccj

        if "wguT" in tap:
            P.dma("gpsimd", tap["wguT"][:, :], wguT[:, :],
                  waits=[("cc", gather_done[("wgu", 3)])])

        Prog.scope = "deq_wd"
        dsc16v = big[:, A_DQ:A_DQ + 5632].bitcast(F16)
        dscE32 = big[:, A_STG:A_STG + 11264].bitcast(F32)
        dg_wgu_done = P.val("dg")
        ev_wgu_done = P.val("ev")
        ac_wgu_done = P.val("ac")
        P.dma("sync", dsc16v.rearrange("p (oc b) -> p oc b", b=352),
              dwn_sc.rearrange("(oc p) b -> oc p b", p=128).transpose([1, 0, 2]),
              waits=[("ev", ev_wgu_done), ("ac", ac_wgu_done)], inc="ldd")
        ds_dsc = P.val("ldd")
        ev_dsc = P.op("vector", lambda e: e.tensor_copy(dscE32, dsc16v),
                      waits=[("ldd", ds_dsc), ("sg0", P.val("sg0")),
                             ("sg1", P.val("sg1"))], inc="ev")
        import os as _os
        SER = int(_os.environ.get("SER_WD", "0"))

        def allw():
            return [(k, P.val(k)) for k in ("ds", "dg", "ev", "ac", "mm")]
        tmp_bf = [scr[:, 2816 + i * 64:2816 + (i + 1) * 64].bitcast(BF16)
                  for i in range(3)]
        tmp_dma = {0: 0, 1: 0, 2: 0}
        wd_pend = []
        dby3 = dby.rearrange("p (oc b) -> p oc b", b=352)
        dsc3 = dscE32.rearrange("p (oc b) -> p oc b", b=352)
        ti = 0
        for par in range(2):
            for h8 in range(2):
                src = dby3[:, h8 * 8:(h8 + 1) * 8, :]
                nib_u8 = dq[4][:, 0:1408].bitcast(U8)
                nib3 = nib_u8.rearrange("p (oc b) -> p oc b", b=352)
                if par == 0:
                    e0 = P.op("vector", lambda e, src=src, nib3=nib3:
                              e.tensor_single_scalar(nib3, src, 15,
                                                     ALU.bitwise_and),
                              waits=[ds_bytes], inc="ev")
                else:
                    e0 = P.op("vector", lambda e, src=src, nib3=nib3:
                              e.tensor_single_scalar(nib3, src, 4,
                                                     ALU.logical_shift_right),
                              waits=[ds_bytes], inc="ev")
                sc3 = dsc3[:, h8 * 8:(h8 + 1) * 8, :]
                wv3 = wv32.rearrange("p (oc b) -> p oc b", b=352)
                evs = decode_batch(nib_u8, 2816, (sc3, wv3), wv32,
                                   pre=[("ev", e0), ("ev", ev_dsc),
                                        ("mm", prev_tr)] +
                                   (allw() if SER in (2, 3) else []))
                for oc8 in range(8):
                    oc = h8 * 8 + oc8
                    pb = p_ax if oc8 % 2 == 0 else p_bc
                    pfree = pax_free if oc8 % 2 == 0 else pbc_free
                    for ss in range(3):
                        cnt = 352 - ss * 128 if ss == 2 else 128
                        wts = [("ev", evs), pfree] if ss == 0 else []
                        if SER != 0 and ss == 0:
                            wts += [("ac", P.val("ac")), ("dg", P.val("dg"))]
                        if SER == 2:
                            wts = wts + allw()
                        tmm = P.op(
                            "tensor",
                            lambda e, oc8=oc8, ss=ss, cnt=cnt, pb=pb:
                            e.transpose(pb[0:cnt, ss * 128:ss * 128 + 128],
                                        wv32[:, oc8 * 352 + ss * 128:
                                             oc8 * 352 + ss * 128 + cnt],
                                        cI32[:]),
                            waits=wts,
                            inc=("mm" if ss == 2 else None))
                    aevl = None
                    for ss in range(3):
                        cnt = 352 - ss * 128 if ss == 2 else 128
                        tsl = ti % 3
                        aevl = P.op(
                            "scalar",
                            lambda e, ss=ss, cnt=cnt, tsl=tsl, pb=pb:
                            e.copy(tmp_bf[tsl][0:cnt, :],
                                   pb[0:cnt, ss * 128:ss * 128 + 128]),
                            waits=[("mm", tmm),
                                   (f"tp{tsl}", tmp_dma[tsl])],
                            inc="ac")
                        wd_pend.append(
                            (par, ss, cnt, oc, tsl))
                        while len(wd_pend) > 2:
                            q = wd_pend.pop(0)
                            tmp_dma[q[4]] = P.dma(
                                "gpsimd",
                                wd_piece[q[0] + 256 * q[1]:
                                         min(q[0] + 256 * q[1] + 2 * q[2],
                                             704):2,
                                         q[3] * 128:(q[3] + 1) * 128],
                                tmp_bf[q[4]][0:q[2], :],
                                waits=[("ac", aevl)], inc=f"tp{q[4]}")
                        ti += 1
                    if oc8 % 2 == 0:
                        pax_free = ("ac", aevl)
                    else:
                        pbc_free = ("ac", aevl)
                prev_tr = tmm
        for _ in range(2):
            aevl = P.op("scalar", lambda e: e.copy(bcast[0:1, 0:4],
                                                   cV[0:1, 0:4]), inc="ac")
        while wd_pend:
            q = wd_pend.pop(0)
            tmp_dma[q[4]] = P.dma(
                "gpsimd",
                wd_piece[q[0] + 256 * q[1]:
                         min(q[0] + 256 * q[1] + 2 * q[2], 704):2,
                         q[3] * 128:(q[3] + 1) * 128],
                tmp_bf[q[4]][0:q[2], :],
                waits=[("ac", aevl)], inc=f"tp{q[4]}")
        dg_wd = P.val("dg")
        cc_wd = P.op("gpsimd", lambda e: e.collective_compute(
            "AllGather", ALU.bypass, replica_groups=RG8,
            ins=[wd_piece[:, :].opt()], outs=[wdT[:, :].opt()]),
            waits=[("tp0", P.val("tp0")), ("tp1", P.val("tp1")),
                   ("tp2", P.val("tp2"))], inc="cc")
        cc_win = P.op("gpsimd", lambda e: e.collective_compute(
            "AllGather", ALU.bypass, replica_groups=RG8,
            ins=[win_piece[:, :].opt()], outs=[winT[:, :].opt()]),
            waits=[("dg", dg_wio)], inc="cc")
        cc_wo = P.op("gpsimd", lambda e: e.collective_compute(
            "AllGather", ALU.bypass, replica_groups=RG8,
            ins=[wo_piece[:, :].opt()], outs=[woT[:, :].opt()]),
            waits=[("dg", dg_wio)], inc="cc")
        if "wdT" in tap:
            P.dma("gpsimd", tap["wdT"][:, :], wdT[:, :],
                  waits=[("cc", cc_wd)])
        if "wdP" in tap:
            P.dma("gpsimd", tap["wdP"][:, :], wd_piece[:, :],
                  waits=[("tp0", P.val("tp0")), ("tp1", P.val("tp1")),
                         ("tp2", P.val("tp2"))])
        if "winT" in tap:
            P.dma("gpsimd", tap["winT"][:, :], winT[:, :],
                  waits=[("cc", cc_win)])

        RINV_D = 1.0 / 2048.0
        pbc_sb_free = ("ev", 0)

        def stats_pass(half, xb_base, tag):
            nonlocal ps_free
            """sum/sumsq over features of xfm half -> p_s1[0:1], p_s2[0:1].

            xb_base: 4 bf16 scratch slots (big cols) for xb / xsq.
            Returns mm value of the last stats matmul."""
            ev_xb = {}
            for c in range(DC):
                sl = c % 2
                xb = big[:, xb_base + sl * 512:xb_base + sl * 512 + 512]
                xq = big[:, xb_base + 1024 + sl * 512:
                         xb_base + 1024 + sl * 512 + 512]
                w = [("mm", mm_stats.get(("s", tag, half, c - 2), 0))]
                P.op("vector", lambda e, c=c, xb=xb:
                     e.tensor_copy(xb, xfm[:, c * T + half * TH:
                                           c * T + half * TH + TH]),
                     waits=w, inc="ev")
                evq = P.op("vector", lambda e, xb=xb, xq=xq:
                           e.tensor_mul(xq, xb, xb), inc="ev")
                P.op("tensor", lambda e, c=c, xb=xb:
                     e.matmul(p_s1[0:1, :], cOnes16[:], xb,
                              start=(c == 0), stop=(c == DC - 1)),
                     waits=[("ev", evq)] + ([ps_free] if c == 0 else []),
                     inc=None)
                m2 = P.op("tensor", lambda e, c=c, xq=xq:
                          e.matmul(p_s2[0:1, :], cOnes16[:], xq,
                                   start=(c == 0), stop=(c == DC - 1)),
                          inc="mm")
                mm_stats[("s", tag, half, c)] = m2
            return m2

        bc_free = pbc_free

        def bcast_row(row_ap, dst_idx, wait_ev):
            nonlocal bc_free, pbc_sb_free
            m = P.op("tensor", lambda e, row_ap=row_ap:
                     e.matmul(p_bc[:, :], cOnesR[:], row_ap,
                              start=True, stop=True),
                     waits=[("ev", wait_ev), bc_free, ds_consts],
                     inc="mm")
            a = P.op("scalar", lambda e, dst_idx=dst_idx:
                     e.copy(bcast[:, dst_idx * TH:(dst_idx + 1) * TH],
                            p_bc[:, :]),
                     waits=[("mm", m), pbc_sb_free], inc="ac")
            bc_free = ("ac", a)
            return a

        mm_stats = {}
        wslot_free = {0: ("cc", 0), 1: ("cc", 0)}
        ps_free = ("ev", 0)

        def ln_rows(eps_col, mm_wait):
            nonlocal ps_free
            ev0 = P.op("vector", lambda e: e.tensor_scalar_mul(
                ROWS[0], p_s1[0:1, :], RINV_D),
                waits=[("mm", mm_wait)], inc="ev")
            evp = P.op("vector", lambda e: e.tensor_scalar_mul(
                ROWS[1], p_s2[0:1, :], RINV_D), inc="ev")
            ps_free = ("ev", evp)
            P.op("vector", lambda e: e.tensor_mul(ROWS[2], ROWS[0], ROWS[0]),
                 inc="ev")
            evv = P.op("vector", lambda e: e.tensor_sub(ROWS[1], ROWS[1],
                                                        ROWS[2]), inc="ev")
            aq = P.op("scalar", lambda e, eps_col=eps_col: e.activation(
                ROWS[2], ROWS[1], AF.Sqrt, bias=cV[0:1, eps_col:eps_col + 1]),
                waits=[("ev", evv)], inc="ac")
            evr = P.op("vector", lambda e: e.reciprocal(ROWS[1], ROWS[2]),
                       waits=[("ac", aq)], inc="ev")
            return evr

        FFN_UPTO = int(_os.environ.get("FFN_UPTO", "4"))

        def ffn_call(ready, tag):
            nonlocal bc_free, pbc_sb_free, ps_free
            last_ev = 0
            for half in range(int(_os.environ.get("FFN_HALVES", "2"))):
                Prog.scope = f"{tag}_h{half}_ln"
                P.op("vector", lambda e: e.engine_nop(), waits=[ready],
                     inc="ev")
                if _os.environ.get("FFN_BARRIER", "0") == "1":
                    aw = [(k, P.val(k)) for k in
                          ("ds", "dg", "ev", "ac", "mm", "ws0", "ws1")]
                    P.op("vector", lambda e: e.engine_nop(), waits=aw,
                         inc="ev")
                    P.op("scalar", lambda e: e.copy(bcast[0:1, 0:4],
                                                    cV[0:1, 0:4]),
                         waits=aw, inc="ac")
                    P.op("tensor", lambda e: e.transpose(
                        p_ax[:, 0:128], cI32[:], cI32[:]), waits=aw, inc="mm")
                mstat = stats_pass(half, A_Y, tag)
                evr = ln_rows(2, mstat)
                a0 = bcast_row(ROWS[0], 0, evr)
                a1 = bcast_row(ROWS[1], 1, evr)
                ev_apply = 0
                for c in range(DC):
                    P.op("vector", lambda e, c=c, half=half: e.tensor_sub(
                        SCR(4), xfm[:, c * T + half * TH:c * T + half * TH + TH],
                        bcast[:, 0:TH]),
                        waits=([("ac", a1)] if c == 0 else ()), inc="ev")
                    ev_apply = P.op("vector", lambda e, c=c: e.tensor_mul(
                        actb[:, c * TH:(c + 1) * TH], SCR(4),
                        bcast[:, TH:2 * TH]), inc="ev")
                pbc_sb_free = ("ev", ev_apply)
                if tag == "ffn1" and half == 0 and "ln1h0" in tap:
                    P.dma("sync", tap["ln1h0"][:, :], actb[:, :],
                          waits=[("ev", ev_apply)])
                if FFN_UPTO <= 1:
                    last_ev = ev_apply
                    continue
                Prog.scope = f"{tag}_h{half}_gate"
                DMAONLY = FFN_UPTO == 15
                silu_done = {}
                h_done = {}
                for c in range(HC):
                    sl = c % 2
                    wofs = sl * 5632
                    wsem, wval = wslot_free[sl]
                    first_use = (tag == "ffn1" and half == 0 and c == 0)
                    dsv = 0
                    for kc in range(DC):
                        ccw = ([("cc", gather_done[("wgu", kc // 4)])]
                               if first_use else [])
                        dsv = P.dma("sync",
                                    wbuf[:, wofs + kc * 256:wofs + kc * 256 + 256],
                                    wguT[kc * 128:(kc + 1) * 128,
                                         256 * c:256 * c + 256],
                                    waits=([(wsem, wval)] if kc == 0 else [])
                                    + ccw,
                                    inc=f"ws{sl}")
                    if DMAONLY:
                        wslot_free[sl] = (f"ws{sl}", dsv)
                        continue
                    gb = p_mm[2 * sl]
                    ub = p_mm[2 * sl + 1]
                    for kc in range(DC):
                        wts = [(f"ws{sl}", dsv), ("ev", ev_apply)] if kc == 0 else []
                        if kc == 0:
                            wts.append(bank_free[2 * sl])
                        mg = P.op("tensor", lambda e, kc=kc, wofs=wofs, gb=gb:
                                  e.matmul(gb[:, :],
                                           wbuf[:, wofs + kc * 256:
                                                wofs + kc * 256 + 128],
                                           actb[:, kc * TH:(kc + 1) * TH],
                                           start=(kc == 0), stop=(kc == DC - 1)),
                                  waits=wts,
                                  inc=("mm" if kc == DC - 1 else None))
                    for kc in range(DC):
                        wts = [bank_free[2 * sl + 1]] if kc == 0 else []
                        mu_ = P.op("tensor", lambda e, kc=kc, wofs=wofs, ub=ub:
                                   e.matmul(ub[:, :],
                                            wbuf[:, wofs + kc * 256 + 128:
                                                 wofs + kc * 256 + 256],
                                            actb[:, kc * TH:(kc + 1) * TH],
                                            start=(kc == 0), stop=(kc == DC - 1)),
                                   waits=wts,
                                   inc=("mm" if kc == DC - 1 else None))
                    wslot_free[sl] = ("mm", mu_)
                    sg = big[:, A_SQ + sl * 512:A_SQ + sl * 512 + 512]
                    asl = P.op("scalar", lambda e, sg=sg, gb=gb:
                               e.activation(sg, gb[:, :], AF.Silu),
                               waits=[("mm", mg), ("ev", h_done.get(c - 2, 0))],
                               inc="ac")
                    bank_free[2 * sl] = ("ac", asl)
                    evh = P.op("vector", lambda e, c=c, sg=sg, ub=ub:
                               e.tensor_mul(big[:, A_H + c * TH:
                                                A_H + (c + 1) * TH],
                                            sg, ub[:, :]),
                               waits=[("ac", asl), ("mm", mu_)], inc="ev")
                    h_done[c] = evh
                    bank_free[2 * sl + 1] = ("ev", evh)
                ev_h_all = P.val("ev")
                if tag == "ffn1" and half == 0 and "hh0" in tap:
                    P.dma("sync", tap["hh0"][:, :],
                          big[:, A_H:A_H + HC * TH], waits=[("ev", ev_h_all)])
                if DMAONLY:
                    last_ev = ev_apply
                    continue
                if FFN_UPTO <= 2:
                    last_ev = ev_h_all
                    continue
                Prog.scope = f"{tag}_h{half}_down"
                ysq_pend = []
                for pr in range(8):
                    slA, slB = 0, 1
                    wsem, wval = wslot_free[0]
                    ds_a = 0
                    for kc in range(22):
                        ds_a = P.dma("sync",
                                     wbuf[:, kc * 256:kc * 256 + 256],
                                     wdT[kc * 128:(kc + 1) * 128,
                                         256 * pr:256 * pr + 256],
                                     waits=([(wsem, wval), ("cc", cc_wd)]
                                            if kc == 0 else ()), inc="ws0")
                    wsem, wval = wslot_free[1]
                    ds_b = 0
                    for kc in range(22, HC):
                        ds_b = P.dma("sync",
                                     wbuf[:, 5632 + (kc - 22) * 256:
                                          5632 + (kc - 22) * 256 + 256],
                                     wdT[kc * 128:(kc + 1) * 128,
                                         256 * pr:256 * pr + 256],
                                     waits=([(wsem, wval)] if kc == 22 else ()),
                                     inc="ws1")
                    b0 = p_mm[2 * (pr % 2)]
                    b1 = p_mm[2 * (pr % 2) + 1]
                    m0 = m1_ = None
                    m_half = None
                    for kc in range(HC):
                        wofs = (kc // 22) * 5632
                        kl = kc % 22
                        rhs = big[:, A_H + kc * TH:A_H + (kc + 1) * TH]
                        wts0 = []
                        if kc == 0:
                            wts0 = [("ws0", ds_a), ("ev", ev_h_all),
                                    bank_free[2 * (pr % 2)]]
                        elif kc == 22:
                            wts0 = [("ws1", ds_b)]
                        m0 = P.op("tensor", lambda e, kl=kl, wofs=wofs, b0=b0,
                                  rhs=rhs, kc=kc:
                                  e.matmul(b0[:, :],
                                           wbuf[:, wofs + kl * 256:
                                                wofs + kl * 256 + 128],
                                           rhs, start=(kc == 0),
                                           stop=(kc == HC - 1)),
                                  waits=wts0,
                                  inc=("mm" if kc == HC - 1 else None))
                        wts1 = [bank_free[2 * (pr % 2) + 1]] if kc == 0 else []
                        m1_ = P.op("tensor", lambda e, kl=kl, wofs=wofs, b1=b1,
                                   rhs=rhs, kc=kc:
                                   e.matmul(b1[:, :],
                                            wbuf[:, wofs + kl * 256 + 128:
                                                 wofs + kl * 256 + 256],
                                            rhs, start=(kc == 0),
                                            stop=(kc == HC - 1)),
                                   waits=wts1,
                                   inc=("mm" if kc == HC - 1 or kc == 21
                                        else None))
                        if kc == 21:
                            m_half = m1_
                    wslot_free[0] = ("mm", m_half)
                    wslot_free[1] = ("mm", m1_)
                    while ysq_pend:
                        oc_, xq_, ev_ = ysq_pend.pop(0)
                        mm_stats[("y", oc_)] = P.op(
                            "tensor", lambda e, xq_=xq_, oc_=oc_:
                            e.matmul(p_s1[0:1, :], cOnes16[:], xq_,
                                     start=(oc_ == 0), stop=(oc_ == 15)),
                            waits=[("ev", ev_)] +
                            ([ps_free] if oc_ == 0 else []), inc="mm")
                    for oi, (bk, mmv) in enumerate(((b0, m0), (b1, m1_))):
                        oc = 2 * pr + oi
                        yt = big[:, A_Y + oc * TH:A_Y + (oc + 1) * TH]
                        ay = P.op("scalar", lambda e, yt=yt, bk=bk:
                                  e.copy(yt, bk[:, :]),
                                  waits=[("mm", mmv)], inc="ac")
                        bank_free[2 * (pr % 2) + oi] = ("ac", ay)
                        xq = big[:, A_SQ + oi * 512:A_SQ + (oi + 1) * 512]
                        evq = P.op("vector", lambda e, yt=yt, xq=xq:
                                   e.tensor_mul(xq, yt, yt),
                                   waits=[("ac", ay),
                                          ("mm", mm_stats.get(("y", oc - 2), 0))],
                                   inc="ev")
                        ysq_pend.append((oc, xq, evq))
                while ysq_pend:
                    oc_, xq_, ev_ = ysq_pend.pop(0)
                    mst = P.op("tensor", lambda e, xq_=xq_, oc_=oc_:
                               e.matmul(p_s1[0:1, :], cOnes16[:], xq_,
                                        start=(oc_ == 0), stop=(oc_ == 15)),
                               waits=[("ev", ev_)], inc="mm")
                    mm_stats[("y", oc_)] = mst
                if FFN_UPTO <= 3:
                    last_ev = P.val("ev")
                    continue
                if tag == "ffn1" and half == 0 and "yh0" in tap:
                    P.dma("sync", tap["yh0"][:, :],
                          big[:, A_Y:A_Y + DC * TH], waits=[("mm", mst)])
                ev0 = P.op("vector", lambda e: e.tensor_scalar_mul(
                    ROWS[0], p_s1[0:1, :], RINV_D),
                    waits=[("mm", mst)], inc="ev")
                ps_free = ("ev", ev0)
                aq = P.op("scalar", lambda e: e.activation(
                    ROWS[1], ROWS[0], AF.Sqrt, bias=cV[0:1, 3:4]),
                    waits=[("ev", ev0)], inc="ac")
                evq = P.op("vector", lambda e: e.reciprocal(ROWS[2], ROWS[1]),
                           waits=[("ac", aq)], inc="ev")
                evw = P.op("vector", lambda e: e.tensor_scalar_mul(
                    ROWS[0], ROWS[2], 0.5), inc="ev")
                a0 = bcast_row(ROWS[0], 0, evw)
                for oc in range(DC):
                    yt = big[:, A_Y + oc * TH:A_Y + (oc + 1) * TH]
                    P.op("vector", lambda e, yt=yt: e.tensor_mul(
                        SCR(4), yt, bcast[:, 0:TH]),
                        waits=([("ac", a0)] if oc == 0 else ()), inc="ev")
                    xsl = xfm[:, oc * T + half * TH:oc * T + half * TH + TH]
                    last_ev = P.op("vector", lambda e, xsl=xsl: e.tensor_add(
                        xsl, xsl, SCR(4)), inc="ev")
                pbc_sb_free = ("ev", last_ev)
                for oc in range(DC):
                    mm_stats.pop(("y", oc), None)
            return ("ev", last_ev)

        def xdump(dst_dram, gate):
            pend = []
            for tc in range(8):
                for dc in range(DC):
                    i = tc * DC + dc
                    b = i % 4
                    sl = i % 3
                    stile = scr[:, (2 + sl) * TH:(2 + sl) * TH + 128]
                    semv = bank_free[b]
                    t = P.op("tensor", lambda e, tc=tc, dc=dc, b=b:
                             e.transpose(p_mm[b][:, 0:128],
                                         xfm[:, dc * T + tc * 128:
                                             dc * T + tc * 128 + 128],
                                         cI32[:]),
                             waits=[gate, semv], inc="mm")
                    a = P.op("scalar",
                             lambda e, b=b, stile=stile:
                             e.copy(stile, p_mm[b][:, 0:128]),
                             waits=[("mm", t),
                                    (f"to{sl}", P.val(f"to{sl}"))],
                             inc="ac")
                    bank_free[b] = ("ac", a)
                    pend.append((tc, dc, sl, stile))
                    while len(pend) > 2:
                        q = pend.pop(0)
                        P.dma("sync",
                              dst_dram[q[0] * 128:(q[0] + 1) * 128,
                                       q[1] * 128:(q[1] + 1) * 128],
                              q[3], waits=[("ac", a)], inc=f"to{q[2]}")
            for _ in range(2):
                a = P.op("scalar", lambda e: e.copy(bcast[0:1, 0:4],
                                                    cV[0:1, 0:4]), inc="ac")
            while pend:
                q = pend.pop(0)
                P.dma("sync",
                      dst_dram[q[0] * 128:(q[0] + 1) * 128,
                               q[1] * 128:(q[1] + 1) * 128],
                      q[3], waits=[("ac", a)], inc=f"to{q[2]}")

        if _os.environ.get("SKIP_FFN", "0") == "1":
            ffn1_done = ("ac", ac_xdone)
        else:
            ffn1_done = ffn_call(("ac", ac_xdone), "ffn1")
        if "ffn1" in tap:
            xdump(tap["ffn1"], ffn1_done)

        def attn_call(ready):
            nonlocal bc_free, pbc_sb_free, ps_free, pax_free
            q_ac = {}
            P.op("vector", lambda e: e.memset(SCR(5)[:, 0:128], 0.0),
                 waits=[ready], inc="ev")
            sqK = scr[:, 9 * TH:9 * TH + TH].bitcast(BF16)[:, 0:512]
            sqV = scr[:, 9 * TH:9 * TH + TH].bitcast(BF16)[:, 512:1024]
            rope = [big[:, A_RP + k * 1024:A_RP + (k + 1) * 1024].bitcast(F32)
                    for k in range(4)]
            rope_user = {k: [] for k in range(4)}
            rope_cnt = {k: 0 for k in range(4)}
            ev_apply_h = {}
            ev_Adone = 0
            for half in range(2):
                Prog.scope = f"attn_h{half}_ln"
                P.op("vector", lambda e: e.engine_nop(), waits=[ready],
                     inc="ev")
                mstat = stats_pass(half, A_KROT, "attn")
                ev0 = P.op("vector", lambda e: e.tensor_scalar_mul(
                    ROWS[0], p_s1[0:1, :], RINV_D),
                    waits=[("mm", mstat)], inc="ev")
                evp = P.op("vector", lambda e: e.tensor_scalar_mul(
                    ROWS[1], p_s2[0:1, :], RINV_D), inc="ev")
                ps_free = ("ev", evp)
                P.op("vector", lambda e: e.tensor_mul(ROWS[2], ROWS[0],
                                                      ROWS[0]), inc="ev")
                evv = P.op("vector", lambda e: e.tensor_sub(
                    ROWS[1], ROWS[1], ROWS[2]), inc="ev")
                aq = P.op("scalar", lambda e: e.activation(
                    ROWS[2], ROWS[1], AF.Sqrt, bias=cV[0:1, 2:3]),
                    waits=[("ev", evv)], inc="ac")
                evr = P.op("vector", lambda e: e.reciprocal(ROWS[2], ROWS[2]),
                           waits=[("ac", aq)], inc="ev")
                a0 = bcast_row(ROWS[0], 0, evr)
                a1 = bcast_row(ROWS[2], 1, evr)
                P.op("vector", lambda e: e.tensor_mul(ROWS[1], ROWS[1],
                                                      ROWS[2]), inc="ev")
                evv2 = P.op("vector", lambda e: e.tensor_mul(
                    ROWS[1], ROWS[1], ROWS[2]), inc="ev")
                aq2 = P.op("scalar", lambda e: e.activation(
                    ROWS[1], ROWS[1], AF.Sqrt, bias=cV[0:1, 2:3]),
                    waits=[("ev", evv2)], inc="ac")
                evr2 = P.op("vector", lambda e: e.reciprocal(ROWS[1], ROWS[1]),
                            waits=[("ac", aq2)], inc="ev")
                evs12 = P.op("vector", lambda e: e.tensor_mul(
                    ROWS[1], ROWS[1], ROWS[2]), inc="ev")
                mbs = P.op("tensor", lambda e: e.matmul(
                    p_bc[:, :], cOnesR[:], ROWS[1], start=True, stop=True),
                    waits=[("ev", evs12), bc_free, ds_consts], inc="mm")
                ev_apply = 0
                for c in range(DC):
                    xsl = xfm[:, c * T + half * TH:c * T + half * TH + TH]
                    P.op("vector", lambda e, xsl=xsl: e.tensor_sub(
                        SCR(0), xsl, bcast[:, 0:TH]),
                        waits=([("ac", a1), ("mm", mbs)] if c == 0 else ()),
                        inc="ev")
                    P.op("vector", lambda e: e.tensor_mul(
                        SCR(1), SCR(0), bcast[:, TH:2 * TH]), inc="ev")
                    P.op("vector", lambda e, xsl=xsl: e.tensor_add(
                        xsl, xsl, SCR(1)), inc="ev")
                    P.op("vector", lambda e: e.tensor_mul(
                        SCR(1), SCR(0), p_bc[:, :]), inc="ev")
                    ev_apply = P.op("vector",
                                    lambda e, c=c: e.tensor_scalar(
                                        actb[:, c * TH:(c + 1) * TH], SCR(1),
                                        gamt[:, c:c + 1], bett[:, c:c + 1],
                                        ALU.mult, ALU.add), inc="ev")
                ev_apply_h[half] = ev_apply
                pbc_sb_free = ("ev", ev_apply)
                bc_free = ("ev", ev_apply)
                Prog.scope = f"attn_h{half}_kv"
                perm_mm = {}
                scr5_rd = 0
                pool_rd = {}
                pool_dflt = P.val("ev")
                for c in range(DC):
                    if c % 2 == 0:
                        for kind, base_oc in (("K", 16 + c), ("V", 32 + c)):
                            sl = (c // 2) % 2 if kind == "K" else \
                                ((c // 2) % 2 + 1) % 2
                            wofs = sl * 5632
                            wsem, wval = wslot_free[sl]
                            dsv = 0
                            for kc in range(DC):
                                dsv = P.dma(
                                    "sync",
                                    wbuf[:, wofs + kc * 256:
                                         wofs + kc * 256 + 256],
                                    winT[kc * 128:(kc + 1) * 128,
                                         base_oc * 128:base_oc * 128 + 256],
                                    waits=([(wsem, wval), ("cc", cc_win)]
                                           if kc == 0 else ()),
                                    inc=f"ws{sl}")
                            if kind == "K":
                                dsv_K, wofs_K = dsv, wofs
                            else:
                                dsv_V, wofs_V = dsv, wofs
                    if c < 8:
                        for which, tbl in ((0, rope_cos), (1, rope_sin)):
                            k = (c % 2) * 2 + which
                            wts = [w for w in rope_user[k]]
                            rope_user[k] = []
                            rope_cnt[k] = P.dma(
                                "gpsimd", rope[k],
                                tbl[c * 128:(c + 1) * 128,
                                    half * TH:half * TH + TH],
                                waits=wts, inc=f"rp{k}")
                    bK = p_mm[(c % 2) * 2]
                    bV = p_mm[(c % 2) * 2 + 1]
                    koff = (c % 2) * 256
                    for kind, bb, oc in (("K", bK, 16 + c), ("V", bV, 32 + c)):
                        wofs = wofs_K if kind == "K" else wofs_V
                        dsv = dsv_K if kind == "K" else dsv_V
                        sub = (c % 2) * 128
                        for kc in range(DC):
                            wts = []
                            if kc == 0:
                                wts = [(f"ws{0 if wofs == 0 else 1}", dsv),
                                       ("ev", ev_apply),
                                       bank_free[(c % 2) * 2 +
                                                 (0 if kind == "K" else 1)]]
                            P.op("tensor",
                                 lambda e, kc=kc, wofs=wofs, sub=sub, bb=bb:
                                 e.matmul(bb[:, :],
                                          wbuf[:, wofs + kc * 256 + sub:
                                               wofs + kc * 256 + sub + 128],
                                          actb[:, kc * TH:(kc + 1) * TH],
                                          start=(kc == 0), stop=(kc == DC - 1)),
                                 waits=wts,
                                 inc=("mm" if kc == DC - 1 else None))
                        mm_qkv = P.cnt["mm"]
                        if c % 2 == 1:
                            wslot_free[0 if wofs == 0 else 1] = \
                                ("mm", mm_qkv)
                        ss = SCR(0) if kind == "K" else SCR(1)
                        aev = P.op("scalar",
                                   lambda e, bb=bb, oc=oc, ss=ss:
                                   e.activation(ss, bb[:, :], AF.Identity,
                                                bias=bias_q[:, oc:oc + 1]),
                                   waits=[("mm", mm_qkv),
                                          ("mm", perm_mm.get(
                                              (kind, c - 1), 0)),
                                          ("ev", kraw_rd.get(
                                              (half, kind, c - 1), 0))],
                                   inc="ac")
                        bank_free[(c % 2) * 2 + (0 if kind == "K" else 1)] = \
                            ("ac", aev)
                        sq = sqK if kind == "K" else sqV
                        evsq = P.op("vector",
                                    lambda e, ss=ss, sq=sq: e.tensor_mul(
                                        sq, ss, ss),
                                    waits=[("ac", aev)] +
                                    ([ps_free] if c == 0 else []), inc="ev")
                        pbank = p_s1 if kind == "K" else p_s2
                        P.op("tensor", lambda e, sq=sq, pbank=pbank, c=c:
                             e.matmul(pbank[0:1, :], cOnes16[:], sq,
                                      start=(c == 0), stop=(c == DC - 1)),
                             waits=[("ev", evsq)],
                             inc=("mm" if c == DC - 1 else None))
                        if c == DC - 1:
                            ss_mm = P.cnt["mm"]
                        dst = big[:, (A_KROT if kind == "K" else A_VROT) +
                                  c * TH:(A_KROT if kind == "K" else A_VROT) +
                                  (c + 1) * TH]
                        if c < 8:
                            pp = p_ax if kind == "K" else p_bc
                            ppfree = pax_free if kind == "K" else bc_free
                            pm = P.op("tensor",
                                      lambda e, ss=ss, pp=pp: e.matmul(
                                          pp[:, :], cPerm[:], ss,
                                          start=True, stop=True),
                                      waits=[("ac", aev), ppfree, ds_consts],
                                      inc="mm")
                            perm_mm[(kind, c)] = pm
                            psc = SCR(2) if kind == "K" else SCR(3)
                            apv = P.op("scalar",
                                       lambda e, pp=pp, psc=psc:
                                       e.copy(psc, pp[:, :]),
                                       waits=[("mm", pm),
                                              ("ev", pool_rd.get(
                                                  (kind, c - 1), pool_dflt))],
                                       inc="ac")
                            if kind == "K":
                                pax_free = ("ac", apv)
                            else:
                                bc_free = ("ac", apv)
                            kcos = (c % 2) * 2
                            ksin = kcos + 1
                            evt = P.op("vector",
                                       lambda e, ss=ss, kcos=kcos:
                                       e.tensor_mul(SCR(4), ss, rope[kcos]),
                                       waits=[(f"rp{kcos}", rope_cnt[kcos]),
                                              ("ac", aev)],
                                       inc="ev")
                            plv = P.op("vector",
                                       lambda e, psc=psc, ksin=ksin:
                                       e.tensor_tensor(SCR(5), psc,
                                                       rope[ksin], ALU.mult),
                                       waits=[("ac", apv),
                                              (f"rp{ksin}", rope_cnt[ksin])],
                                       inc="ev")
                            pool_rd[(kind, c)] = plv
                            evo = P.op("vector",
                                       lambda e, dst=dst: e.tensor_add(
                                           dst, SCR(4), SCR(5)), inc="ev")
                            scr5_rd = evo
                            rope_user[kcos].append(("ev", evo))
                            rope_user[ksin].append(("ev", evo))
                            kraw_rd[(half, kind, c)] = evo
                        else:
                            evo = P.op("vector",
                                       lambda e, ss=ss, dst=dst:
                                       e.tensor_copy(dst, ss),
                                       waits=[("ac", aev)], inc="ev")
                            kraw_rd[(half, kind, c)] = evo
                ev_kv = P.val("ev")
                Prog.scope = f"attn_h{half}_A"
                evA = P.op("vector", lambda e: e.tensor_scalar_mul(
                    ROWS[0], p_s1[0:1, :], 1.0),
                    waits=[("mm", ss_mm)], inc="ev")
                evB = P.op("vector", lambda e: e.tensor_mul(
                    ROWS[1], ROWS[0], p_s2[0:1, :]), inc="ev")
                ps_free = ("ev", evB)
                aqw = P.op("scalar", lambda e: e.activation(
                    ROWS[2], ROWS[1], AF.Sqrt),
                    waits=[("ev", evB)], inc="ac")
                evw = P.op("vector", lambda e: e.reciprocal(ROWS[2], ROWS[2]),
                           waits=[("ac", aqw)], inc="ev")
                a0 = bcast_row(ROWS[2], 0, evw)
                at = atile if half == 0 else atile2
                evred = 0
                for c in range(DC):
                    kk = big[:, A_KROT + c * TH:A_KROT + (c + 1) * TH]
                    vv = big[:, A_VROT + c * TH:A_VROT + (c + 1) * TH]
                    P.op("vector", lambda e, kk=kk, vv=vv: e.tensor_mul(
                        SCR(2), kk, vv),
                        waits=([("ac", a0)] if c == 0 else ()), inc="ev")
                    P.op("vector", lambda e: e.tensor_mul(
                        SCR(2), SCR(2), bcast[:, 0:TH]), inc="ev")
                    evred = P.op("vector", lambda e, at=at, c=c: e.reduce_sum(
                        at[:, c:c + 1], SCR(2), axis=AX), inc="ev")
                pbc_sb_free = ("ev", evred)
                Prog.scope = f"attn_h{half}_q"
                for c in range(DC):
                    if c % 2 == 0:
                        sl = (c // 2) % 2
                        wofs = sl * 5632
                        wsem, wval = wslot_free[sl]
                        dsv = 0
                        for kc in range(DC):
                            dsv = P.dma(
                                "sync",
                                wbuf[:, wofs + kc * 256:wofs + kc * 256 + 256],
                                winT[kc * 128:(kc + 1) * 128,
                                     c * 128:c * 128 + 256],
                                waits=([(wsem, wval)] if kc == 0 else ()),
                                inc=f"ws{sl}")
                    sl = (c // 2) % 2
                    wofs = sl * 5632
                    sub = (c % 2) * 128
                    bb = p_mm[c % 4]
                    for kc in range(DC):
                        wts = []
                        if kc == 0:
                            wts = [(f"ws{sl}", dsv), bank_free[c % 4]]
                        mq = P.op("tensor",
                                  lambda e, kc=kc, wofs=wofs, sub=sub, bb=bb:
                                  e.matmul(bb[:, :],
                                           wbuf[:, wofs + kc * 256 + sub:
                                                wofs + kc * 256 + sub + 128],
                                           actb[:, kc * TH:(kc + 1) * TH],
                                           start=(kc == 0),
                                           stop=(kc == DC - 1)),
                                  waits=wts,
                                  inc=("mm" if kc == DC - 1 else None))
                    if c % 2 == 1:
                        wslot_free[sl] = ("mm", mq)
                    qd = big[:, A_QY + c * T + half * TH:
                             A_QY + c * T + half * TH + TH]
                    aq_ = P.op("scalar", lambda e, bb=bb, c=c, qd=qd:
                               e.activation(qd, bb[:, :], AF.Identity,
                                            bias=bias_q[:, c:c + 1]),
                               waits=[("mm", mq)], inc="ac")
                    bank_free[c % 4] = ("ac", aq_)
                    q_ac[(half, c)] = aq_
                if half == 0 and _os.environ.get("ATTN_STOP", "0") == "1":
                    evx = P.val("ev")
                    acx = P.val("ac")
                    if "xn" in tap:
                        P.dma("gpsimd", tap["xn"][:, :], actb[:, :],
                              waits=[("ev", evx), ("ac", acx)])
                    if "kr" in tap:
                        P.dma("gpsimd", tap["kr"][:, :],
                              big[:, A_KROT:A_KROT + 8192],
                              waits=[("ev", evx), ("ac", acx)])
                    if "vr" in tap:
                        P.dma("gpsimd", tap["vr"][:, :],
                              big[:, A_VROT:A_VROT + 8192],
                              waits=[("ev", evx), ("ac", acx)])
                    if "qy" in tap:
                        P.dma("gpsimd", tap["qy"][:, :],
                              big[:, A_QY:A_QY + 16384],
                              waits=[("ev", evx), ("ac", acx)])
                    if "at" in tap:
                        P.dma("gpsimd", tap["at"][:, :], atile[:, 0:16],
                              waits=[("ev", evx), ("ac", acx)])
                    return ("ev", evx)
                if half == 1 and _os.environ.get("ATTN_STOP2", "0") == "1":
                    evx = P.val("ev")
                    acx = P.val("ac")
                    if "kr" in tap:
                        P.dma("gpsimd", tap["kr"][:, :],
                              big[:, A_KROT:A_KROT + 8192],
                              waits=[("ev", evx), ("ac", acx)])
                    if "vr" in tap:
                        P.dma("gpsimd", tap["vr"][:, :],
                              big[:, A_VROT:A_VROT + 8192],
                              waits=[("ev", evx), ("ac", acx)])
                    if "at" in tap:
                        P.dma("gpsimd", tap["at"][:, :], atile2[:, :],
                              waits=[("ev", evx), ("ac", acx)])
                    if "xn" in tap:
                        P.dma("gpsimd", tap["xn"][:, 0:32].bitcast(F32),
                              atile[:, 0:16],
                              waits=[("ev", evx), ("ac", acx)])
                    return ("ev", evx)
                if half == 1:
                    for _ in range(2):
                        P.op("vector", lambda e: e.tensor_scalar_mul(
                            SCR(2), SCR(2), 1.0), inc="ev")
                    ev_Adone = P.val("ev")
            Prog.scope = "attn_allg"
            A_LOCAL = _os.environ.get("A_LOCAL", "0") == "1"
            if A_LOCAL:
                P.op("vector", lambda e: e.tensor_add(
                    SCR(5)[:, 128:144], atile[:, 0:16], atile2[:, 0:16]),
                    waits=[("ev", ev_Adone)], inc="ev")
                evAp = P.op("vector", lambda e: e.tensor_scalar(
                    SCR(5)[:, 160:176], SCR(5)[:, 128:144], 0.5, 0.5,
                    ALU.mult, ALU.add), inc="ev")
            else:
                P.op("vector", lambda e: e.tensor_add(
                    SCR(5)[:, 0:16], atile2[:, 0:16], atile[:, 0:16]),
                    waits=[("ev", ev_Adone)], inc="ev")
                P.op("vector", lambda e: e.tensor_add(
                    SCR(5)[:, 64:80], atile2[:, 0:16], atile[:, 0:16]),
                    inc="ev")
                for _ in range(2):
                    ev_Adone2 = P.op("vector", lambda e: e.tensor_scalar_mul(
                        SCR(2), SCR(2), 1.0), inc="ev")
                if "vr" in tap:
                    P.dma("gpsimd", tap["vr"][:, 0:256],
                          SCR(5)[:, 0:128].bitcast(BF16),
                          waits=[("ev", ev_Adone2)])
                P.dma("gpsimd", arA_in, SCR(5)[:, 0:128],
                      waits=[("ev", ev_Adone2)], inc="arA")
                ccA = P.op("gpsimd", lambda e: e.collective_compute(
                    "AllGather", ALU.bypass, replica_groups=RG8,
                    ins=[arA_in.opt()], outs=[arA_out.opt()]),
                    waits=[("arA", P.val("arA"))], inc="cc")
                plA = P.op("gpsimd", lambda e: e.tensor_copy(
                    big[:, A_RP:A_RP + 64], big[:, A_RP:A_RP + 64]),
                    waits=[("cc", ccA)], inc="pl")
                P.dma("gpsimd",
                      scr[:, 3 * TH:5 * TH].rearrange("p (r c) -> p r c",
                                                      c=128),
                      arA_out.rearrange("(r p) c -> p r c", p=128),
                      waits=[("pl", plA)], inc="arA")
                arA_t = P.val("arA")
                for _ in range(2):
                    P.op("vector", lambda e: e.tensor_scalar_mul(
                        SCR(2), SCR(2), 1.0), waits=[("arA", arA_t)],
                        inc="ev")
                B = 3 * TH
                for i in range(4):
                    P.op("vector", lambda e, i=i: e.tensor_add(
                        SCR(5)[:, 176 + 16 * i:192 + 16 * i],
                        scr[:, B + 256 * i:B + 256 * i + 16],
                        scr[:, B + 256 * i + 128:B + 256 * i + 144]),
                        inc="ev")
                P.op("vector", lambda e: e.tensor_add(
                    SCR(5)[:, 128:144], SCR(5)[:, 176:192],
                    SCR(5)[:, 192:208]), inc="ev")
                P.op("vector", lambda e: e.tensor_add(
                    SCR(5)[:, 144:160], SCR(5)[:, 208:224],
                    SCR(5)[:, 224:240]), inc="ev")
                for _ in range(2):
                    P.op("vector", lambda e: e.tensor_scalar_mul(
                        SCR(2), SCR(2), 1.0), inc="ev")
                P.op("vector", lambda e: e.tensor_add(
                    SCR(5)[:, 240:256], SCR(5)[:, 128:144],
                    SCR(5)[:, 144:160]), inc="ev")
                for _ in range(2):
                    P.op("vector", lambda e: e.tensor_scalar_mul(
                        SCR(2), SCR(2), 1.0), inc="ev")
                evAp = P.op("vector", lambda e: e.tensor_scalar(
                    SCR(5)[:, 160:176], SCR(5)[:, 240:256], 0.5, 0.5,
                    ALU.mult, ALU.add), inc="ev")
                if "xn" in tap:
                    P.dma("gpsimd", tap["xn"][:, 0:2048],
                          scr[:, 3 * TH:5 * TH].bitcast(BF16),
                          waits=[("ev", evAp)])
            if "at" in tap and _os.environ.get("ATTN_STOP", "0") != "1":
                evsl = P.op("vector", lambda e: e.tensor_copy(
                    atile[:, 0:16], SCR(5)[:, 160:176]),
                    waits=[("ev", evAp)], inc="ev")
                for _ in range(2):
                    evsl = P.op("vector", lambda e: e.tensor_scalar_mul(
                        SCR(2), SCR(2), 1.0), inc="ev")
                P.dma("gpsimd", tap["at"][:, :], atile[:, 0:16],
                      waits=[("ev", evsl)])
            Prog.scope = "attn_wout"
            evY = {}
            for half in range(2):
                for c in range(DC):
                    qd = big[:, A_QY + c * T + half * TH:
                             A_QY + c * T + half * TH + TH]
                    evY[(half, c)] = P.op(
                        "vector", lambda e, qd=qd, c=c: e.tensor_scalar(
                            qd, qd, SCR(5)[:, 160 + c:161 + c], None,
                            ALU.mult),
                        waits=([("ev", evAp)] if (half, c) == (0, 0) else []) +
                        [("ac", q_ac[(half, c)])],
                        inc="ev")
                for _ in range(2):
                    evY[(half, DC - 1)] = P.op(
                        "vector", lambda e: e.tensor_scalar_mul(
                            SCR(2), SCR(2), 1.0), inc="ev")
            if "qy" in tap and _os.environ.get("ATTN_STOP", "0") != "1":
                P.dma("gpsimd", tap["qy"][:, :], big[:, A_QY:A_QY + 16384],
                      waits=[("ev", evY[(1, DC - 1)])])
            if "woT" in tap:
                P.dma("gpsimd", tap["woT"][:, :], woT[:, :],
                      waits=[("cc", cc_wo)])
            last_ev = 0
            for half in range(2):
                for c in range(DC):
                    if c % 2 == 0:
                        sl = (c // 2) % 2
                        wofs = sl * 5632
                        wsem, wval = wslot_free[sl]
                        dsv = 0
                        for kc in range(DC):
                            dsv = P.dma(
                                "sync",
                                wbuf[:, wofs + kc * 256:wofs + kc * 256 + 256],
                                woT[kc * 128:(kc + 1) * 128,
                                    c * 128:c * 128 + 256],
                                waits=([(wsem, wval), ("cc", cc_wo)]
                                       if kc == 0 else ()),
                                inc=f"ws{sl}")
                    sl = (c // 2) % 2
                    wofs = sl * 5632
                    sub = (c % 2) * 128
                    bb = p_mm[c % 4]
                    for kc in range(DC):
                        wts = []
                        if kc == 0:
                            wts = [(f"ws{sl}", dsv), bank_free[c % 4],
                                   ("ev", evY[(half, DC - 1)])]
                        mo = P.op("tensor",
                                  lambda e, kc=kc, wofs=wofs, sub=sub, bb=bb,
                                  half=half:
                                  e.matmul(bb[:, :],
                                           wbuf[:, wofs + kc * 256 + sub:
                                                wofs + kc * 256 + sub + 128],
                                           big[:, A_QY + kc * T + half * TH:
                                               A_QY + kc * T + half * TH + TH],
                                           start=(kc == 0),
                                           stop=(kc == DC - 1)),
                                  waits=wts,
                                  inc=("mm" if kc == DC - 1 else None))
                    if c % 2 == 1:
                        wslot_free[sl] = ("mm", mo)
                    ao = P.op("scalar", lambda e, bb=bb, c=c:
                              e.activation(SCR(4), bb[:, :], AF.Identity,
                                           bias=bias_o[:, c:c + 1]),
                              waits=[("mm", mo), ("ev", last_ev)],
                              inc="ac")
                    bank_free[c % 4] = ("ac", ao)
                    xsl = xfm[:, c * T + half * TH:c * T + half * TH + TH]
                    last_ev = P.op("vector", lambda e, xsl=xsl: e.tensor_add(
                        xsl, xsl, SCR(4)), waits=[("ac", ao)], inc="ev")
            return ("ev", last_ev)

        kraw_rd = {}
        if _os.environ.get("SKIP_ATTN", "0") == "1":
            attn_done = ffn1_done
        else:
            attn_done = attn_call(ffn1_done)
        if "attn" in tap:
            xdump(tap["attn"], attn_done)

        if _os.environ.get("SKIP_FFN2", "0") == "1":
            ffn2_done = attn_done
        else:
            ffn2_done = ffn_call(attn_done, "ffn2")

        Prog.scope = "xstore"
        pend = []
        for tc in range(8):
            for dc in range(DC):
                i = tc * DC + dc
                b = i % 4
                sl = i % 3
                stile = scr[:, (2 + sl) * TH:(2 + sl) * TH + 128]
                semv = bank_free[b]
                t = P.op("tensor", lambda e, tc=tc, dc=dc, b=b:
                         e.transpose(p_mm[b][:, 0:128],
                                     xfm[:, dc * T + tc * 128:
                                         dc * T + tc * 128 + 128],
                                     cI32[:]),
                         waits=[ffn2_done, semv], inc="mm")
                a = P.op("scalar",
                         lambda e, b=b, stile=stile:
                         e.copy(stile, p_mm[b][:, 0:128]),
                         waits=[("mm", t), (f"to{sl}", P.val(f"to{sl}"))],
                         inc="ac")
                bank_free[b] = ("ac", a)
                pend.append((tc, dc, sl, stile))
                while len(pend) > 2:
                    q = pend.pop(0)
                    P.dma("sync",
                          y_out[q[0] * 128:(q[0] + 1) * 128,
                                q[1] * 128:(q[1] + 1) * 128],
                          q[3], waits=[("ac", a)], inc=f"to{q[2]}")
        for _ in range(2):
            a = P.op("scalar", lambda e: e.copy(bcast[0:1, 0:4],
                                                cV[0:1, 0:4]), inc="ac")
        while pend:
            q = pend.pop(0)
            P.dma("sync",
                  y_out[q[0] * 128:(q[0] + 1) * 128,
                        q[1] * 128:(q[1] + 1) * 128],
                  q[3], waits=[("ac", a)], inc=f"to{q[2]}")

        P.emit(block)
        fin_vals = [(k, P.val(k)) for k in list(P.cnt.keys())]

        def fin(eng):
            for k, v in fin_vals:
                if v:
                    eng.wait_ge(P.sems[k], v)
        block.sync(fin)
    return nc


def host_inputs(L0_attn_X, L0_attn_W_in, L0_attn_b_in, L0_attn_W_out,
                L0_attn_b_out, L0_attn_gamma, L0_attn_beta,
                L0_ffn_gate_up_packed, L0_ffn_gate_up_scales,
                L0_ffn_down_packed, L0_ffn_down_scales):
    X = np.ascontiguousarray(np.asarray(L0_attn_X, np.float32).reshape(8192, D))
    W_in = np.asarray(L0_attn_W_in, np.float32)
    W_out = np.asarray(L0_attn_W_out, np.float32)
    gupB = np.ascontiguousarray(np.asarray(L0_ffn_gate_up_packed, np.int32)) \
        .view(np.uint8)
    dwnB = np.ascontiguousarray(np.asarray(L0_ffn_down_packed, np.int32)) \
        .view(np.uint8)
    gsc = np.asarray(L0_ffn_gate_up_scales)
    dsc = np.asarray(L0_ffn_down_scales)

    ident = np.eye(128, dtype=np.float32)
    perm = np.zeros((128, 128), np.float32)
    for jj in range(64):
        perm[2 * jj + 1, 2 * jj] = -1.0
        perm[2 * jj, 2 * jj + 1] = 1.0
    onesrow = np.ones((1, 128), np.float32)
    ones16 = np.ones((128, 1), BF)
    cvals = np.zeros((128, 4), np.float32)
    cvals[:, 0] = -4.0
    cvals[:, 1] = -6.0
    cvals[:, 2] = EPS_LN
    cvals[:, 3] = EPS_RMS

    inv = 1.0 / (10000.0 ** (np.arange(0, 1024, 2, dtype=np.float64) / 1024.0))
    in_maps = []
    for r in range(8):
        pos = np.arange(r * T, (r + 1) * T, dtype=np.float64)
        fr = np.outer(pos, inv)
        cos = np.repeat(np.cos(fr), 2, 1).T.astype(np.float32)
        sin = np.repeat(np.sin(fr), 2, 1).T.astype(np.float32)
        gup_sl = np.ascontiguousarray(np.concatenate(
            [gupB[:, j * 256 + r * 32: j * 256 + r * 32 + 32]
             for j in range(4)], axis=1))
        gsc_sl = np.ascontiguousarray(
            gsc[[4 * j + r // 2 for j in range(4)], :]).reshape(352, 128)
        dwn_sl = np.ascontiguousarray(dwnB[:, r * 352:(r + 1) * 352])
        grow = (704 * r + 2 * np.arange(352)) // 128
        dsc_exp = np.ascontiguousarray(dsc[grow, :].T)
        in_maps.append({
            "x_in": np.ascontiguousarray(X[r * T:(r + 1) * T]),
            "win_sl": np.ascontiguousarray(W_in[r * 256:(r + 1) * 256]),
            "wout_sl": np.ascontiguousarray(W_out[r * 256:(r + 1) * 256]),
            "b_in": np.asarray(L0_attn_b_in, np.float32).reshape(48, 128),
            "b_out": np.asarray(L0_attn_b_out, np.float32).reshape(16, 128),
            "gam_in": np.asarray(L0_attn_gamma, np.float32).reshape(16, 128),
            "bet_in": np.asarray(L0_attn_beta, np.float32).reshape(16, 128),
            "gup_by": gup_sl,
            "gup_sc": gsc_sl,
            "dwn_by": dwn_sl,
            "dwn_sc": dsc_exp,
            "rope_cos": np.ascontiguousarray(cos),
            "rope_sin": np.ascontiguousarray(sin),
            "ident_f32": ident,
            "perm_f32": perm,
            "onesrow_f32": onesrow,
            "ones_bf16": ones16,
            "cvals": cvals,
        })
    return in_maps


def kernel(**inputs):
    in_maps = host_inputs(**inputs)
    nc = build_program()
    res = run_bass_kernel_spmd(nc, in_maps, core_ids=list(range(8)))
    out = np.concatenate([res.results[r]["y_out"] for r in range(8)], axis=0)
    return out.reshape(1, 8192, D)
